# version 05
# baseline (speedup 1.0000x reference)
import contextlib

import numpy as np
import ml_dtypes
import concourse.bass as bass
import concourse.mybir as mybir
from concourse.bass_utils import run_bass_kernel_spmd

F32 = mybir.dt.float32
BF16 = mybir.dt.bfloat16
AL = mybir.AluOpType
AF = mybir.ActivationFunctionType
AX = mybir.AxisListType

NCORES = 8
T = 4096
D = 1024
TILE = 512
NT = T // TILE
NSUB = TILE // 128
KC = D // 128
NE = 6
LN_EPS = 1e-5
DEN_EPS = 1e-6
RMS_EPS = 1e-6
BIG = 3.0e38

ENGS = ("tensor", "vector", "scalar", "sync", "gpsimd")
DSEMS = ("dwa", "dwb", "dra", "drb", "dsa", "dsb", "dxi", "dxo", "dmi")
ALLSEMS = ENGS + DSEMS


class Prog:

    def __init__(self):
        self.q = {e: [] for e in ENGS}
        self.cnt = {e: 0 for e in ALLSEMS}

    def wait(self, eng, src, val):
        if val <= 0:
            return
        assert val <= self.cnt[src], f"forward wait {eng}<-{src} {val}>{self.cnt[src]}"
        self.q[eng].append(("w", src, val))

    def op(self, eng, fn, sig=0, sem=None):
        sem = sem or eng
        self.q[eng].append(("o", fn, sig, sem))
        if sig:
            self.cnt[sem] += sig
        return self.cnt[sem]


class _StopRecording(Exception):
    pass


def build(nt=NT, taps=(), stop_after=None, no_scratch=False):
    taps = list(taps)
    if stop_after is not None:
        if stop_after.startswith("!"):
            stop_after = stop_after[1:]
        elif stop_after not in taps:
            taps.append(stop_after)
    nc = bass.Bass("TRN2", target_bir_lowering=False, detect_race_conditions=False)
    p = Prog()

    x_d = nc.dram_tensor("x", [T, D], F32, kind="ExternalInput").ap()
    w_qkv = nc.dram_tensor("w_qkv", [D, 3 * D], F32, kind="ExternalInput").ap()
    b_qkv = nc.dram_tensor("b_qkv", [3 * D], F32, kind="ExternalInput").ap()
    w_gate = nc.dram_tensor("w_gate", [D, D], F32, kind="ExternalInput").ap()
    b_gate = nc.dram_tensor("b_gate", [D], F32, kind="ExternalInput").ap()
    w_proj = nc.dram_tensor("w_proj", [D, D], F32, kind="ExternalInput").ap()
    b_proj = nc.dram_tensor("b_proj", [D], F32, kind="ExternalInput").ap()
    ln_g = nc.dram_tensor("ln_g", [D], F32, kind="ExternalInput").ap()
    ln_b = nc.dram_tensor("ln_b", [D], F32, kind="ExternalInput").ap()
    router_W = nc.dram_tensor("router_W", [2, D, NE], F32, kind="ExternalInput").ap()
    router_b = nc.dram_tensor("router_b", [2, NE], F32, kind="ExternalInput").ap()
    gate_bias = nc.dram_tensor("gate_bias", [2, NE], F32, kind="ExternalInput").ap()
    exp_Wup = nc.dram_tensor("exp_Wup", [NE, D, D], F32, kind="ExternalInput").ap()
    exp_bup = nc.dram_tensor("exp_bup", [NE, D], F32, kind="ExternalInput").ap()
    exp_Wg = nc.dram_tensor("exp_Wg", [NE, D, D], F32, kind="ExternalInput").ap()
    exp_bg = nc.dram_tensor("exp_bg", [NE, D], F32, kind="ExternalInput").ap()
    exp_Wdown = nc.dram_tensor("exp_Wdown", [NE, D, D], F32, kind="ExternalInput").ap()
    exp_bdown = nc.dram_tensor("exp_bdown", [NE, D], F32, kind="ExternalInput").ap()
    share_Wup = nc.dram_tensor("share_Wup", [D, 2 * D], F32, kind="ExternalInput").ap()
    share_bup = nc.dram_tensor("share_bup", [2 * D], F32, kind="ExternalInput").ap()
    share_Wg = nc.dram_tensor("share_Wg", [D, D], F32, kind="ExternalInput").ap()
    share_bg = nc.dram_tensor("share_bg", [D], F32, kind="ExternalInput").ap()
    share_Wdown = nc.dram_tensor("share_Wdown", [D, D], F32, kind="ExternalInput").ap()
    share_bdown = nc.dram_tensor("share_bdown", [D], F32, kind="ExternalInput").ap()
    norm_w = nc.dram_tensor("norm_w", [2, D], F32, kind="ExternalInput").ap()
    out_d = nc.dram_tensor("out", [T, D], F32, kind="ExternalOutput").ap()

    ident_b = nc.inline_tensor(np.eye(128, dtype=ml_dtypes.bfloat16), "identb").ap()
    ident_f = nc.inline_tensor(np.eye(128, dtype=np.float32), "identf").ap()
    mrows = np.arange(128)[:, None]
    mcols = np.arange(896)[None, :]
    mask_np = (mcols >= mrows + 384).astype(ml_dtypes.bfloat16)
    mask_c = nc.inline_tensor(mask_np, "maskc").ap()

    def chunked(ap2d):
        return ap2d.rearrange("(c p) f -> p c f", p=128)

    WMAT = {}
    WMAT["wgate"] = chunked(w_gate)
    WMAT["wq"] = chunked(w_qkv[:, 0:D])
    WMAT["wk"] = chunked(w_qkv[:, D:2 * D])
    WMAT["wv"] = chunked(w_qkv[:, 2 * D:3 * D])
    WMAT["wproj"] = chunked(w_proj)
    for e in range(NE):
        WMAT[f"up{e}"] = chunked(exp_Wup[e])
        WMAT[f"wg{e}"] = chunked(exp_Wg[e])
        WMAT[f"dn{e}"] = chunked(exp_Wdown[e])
    WMAT["sup0"] = chunked(share_Wup[:, 0:D])
    WMAT["sup1"] = chunked(share_Wup[:, D:2 * D])
    WMAT["sg"] = chunked(share_Wg)
    WMAT["sdn"] = chunked(share_Wdown)
    WSCR = {k: nc.dram_tensor(f"wscr_{k}", [128, KC * D], BF16, kind="Internal").ap()
            for k in WMAT}

    tap_out = {}
    ctx = contextlib.ExitStack()
    with ctx:
        sb = lambda name, shape, dt: ctx.enter_context(nc.sbuf_tensor(name, shape, dt))
        X = [sb("X0", [128, 4 * D], F32), sb("X1", [128, 4 * D], F32)]
        XF = sb("XF", [128, 4 * D], F32)
        S = sb("S", [128, KC * D], F32)
        KS = sb("KS", [128, KC], F32)
        KSB = sb("KSB", [128, KC], BF16)
        BClng = sb("BClng", [128, D], F32)
        BClnb = sb("BClnb", [128, D], F32)
        BCnw = [sb("BCnw0", [128, D], F32), sb("BCnw1", [128, D], F32)]
        BCbproj = sb("BCbproj", [128, D], F32)
        BCbv = sb("BCbv", [128, D], F32)
        BCsbd2 = sb("BCsbd2", [128, D], F32)
        RB = sb("RB", [128, 12], F32)
        GB = sb("GB", [128, 12], F32)
        BQKV = sb("BQKV", [128, 16], F32)
        BGATE = sb("BGATE", [128, KC], F32)
        BUP = sb("BUP", [128, NE * KC], F32)
        BGE = sb("BGE", [128, NE * KC], F32)
        SBUP = sb("SBUP", [128, 2 * KC], F32)
        SBG = sb("SBG", [128, KC], F32)
        BDR = sb("BDR", [NE, D], F32)
        BDRB = sb("BDRB", [NE, D], BF16)
        RW = sb("RW", [128, 2 * KC * NE], BF16)
        SCR = sb("SCR", [128, 64], F32)
        G8 = sb("G8", [128, 32], F32)
        BIAS8 = sb("BIAS8", [128, 32], F32)
        M8 = sb("M8", [128, 32], F32)
        SEL = sb("SEL", [128, 24], F32)
        GSEL = sb("GSEL", [128, 24], F32)
        W6 = sb("W6", [128, 24], F32)
        EL1 = sb("EL1", [128, 512], F32)
        EL2 = sb("EL2", [128, 512], F32)
        GS = sb("GS", [128, 1024], F32)
        WT = sb("WT", [NE, 4 * 128], BF16)
        IDB = sb("IDB", [128, 128], BF16)
        IDF = sb("IDF", [128, 128], F32)
        MASK = sb("MASK", [128, 896], BF16)
        ONES1 = sb("ONES1", [128, 1], BF16)
        CONSTS = sb("CONSTS", [128, 2], F32)
        B1 = sb("B1", [128, KC * 512], BF16)
        B2 = sb("B2", [128, KC * 512], BF16)
        B3 = sb("B3", [128, KC * 512], BF16)
        B4 = sb("B4", [128, 4 * D], BF16)
        B5 = sb("B5", [128, 4 * D], BF16)
        SC = sb("SC", [128, 4 * 512], BF16)
        W0 = sb("W0", [128, KC * D], BF16)
        W1 = sb("W1", [128, KC * D], BF16)
        WSLOT = [W0, W1]

        PT0 = ctx.enter_context(nc.psum_tensor("PT0", [128, 1024], F32))
        PT1 = ctx.enter_context(nc.psum_tensor("PT1", [128, 1024], F32))
        PF0 = ctx.enter_context(nc.psum_tensor("PF0", [128, 512], F32))
        PF1 = ctx.enter_context(nc.psum_tensor("PF1", [128, 512], F32))
        PF2 = ctx.enter_context(nc.psum_tensor("PF2", [128, 512], F32))
        PSM = ctx.enter_context(nc.psum_tensor("PSM", [128, 512], F32))
        PTR = PSM[:, 0:256].bitcast(BF16)
        PDEN = PSM[:, 256:260]
        PRT = PSM[:, 260:284]
        PW6 = PSM[0:NE, 284:412]

        sems = {e: ctx.enter_context(nc.semaphore(f"sem_{e}")) for e in ALLSEMS}

        regions = {k: {"ap": a, "evac": None} for k, a in
                   [("PT0", PT0), ("PT1", PT1), ("PF0", PF0), ("PF1", PF1),
                    ("PF2", PF2), ("PTR", PTR), ("PDEN", PDEN), ("PRT", PRT),
                    ("PW6", PW6)]}

        def reg_start(name):
            r = regions[name]
            if r["evac"] is not None:
                p.wait("tensor", *r["evac"])
            return r["ap"]

        def reg_evac_done(name, eng):
            regions[name]["evac"] = (eng, p.cnt[eng])

        def t(fn, sig=0):
            return p.op("tensor", fn, sig)

        def v(fn, sig=0):
            return p.op("vector", fn, sig)

        def s(fn, sig=0):
            return p.op("scalar", fn, sig)

        def g(fn, sig=0):
            return p.op("gpsimd", fn, sig)

        def tap(lbl, ap, prod_eng):
            if lbl not in taps:
                if lbl == stop_after:
                    raise _StopRecording()
                return
            shp = list(ap.shape)
            t_d = nc.dram_tensor(f"tap_{lbl}", shp, F32, kind="ExternalOutput").ap()
            tap_out[lbl] = f"tap_{lbl}"
            p.wait("gpsimd", prod_eng, p.cnt[prod_eng])
            gv = p.op("gpsimd", lambda b, a=ap, o=t_d: b.dma_start(out=o, in_=a), sig=16, sem="dmi")
            for e in ("tensor", "vector", "scalar", "sync"):
                p.wait(e, "dmi", gv)
            if lbl == stop_after:
                raise _StopRecording()

        wuse_idx = [0]
        wdone = {}
        wwrite = {}
        wwrite_name = {}
        wcast = set()

        def w_use(name):
            idx = wuse_idx[0]
            wuse_idx[0] += 1
            slot = WSLOT[idx % 2]
            dw = ("dwa", "dwb")[idx % 2]
            dr = ("dra", "drb")[idx % 2]
            ds = ("dsa", "dsb")[idx % 2]
            if name not in wcast or no_scratch:
                if not no_scratch:
                    wcast.add(name)
                if idx - 2 in wdone:
                    p.wait("gpsimd", "tensor", wdone[idx - 2])
                if idx - 2 in wwrite:
                    p.wait("gpsimd", ds, wwrite[idx - 2])
                src = WMAT[name]
                gv = p.op("gpsimd", lambda b, sl=slot, i=src:
                          b.dma_start(out=sl[:, :].rearrange("p (c f) -> p c f", c=KC), in_=i),
                          sig=16, sem=dw)
                p.wait("tensor", dw, gv)
                if not no_scratch:
                    p.wait("sync", dw, gv)
                    wwrite[idx] = p.op("sync", lambda b, sl=slot, o=WSCR[name]:
                                       b.dma_start(out=o, in_=sl[:, :]), sig=16, sem=ds)
                    wwrite_name[name] = (ds, wwrite[idx])
            else:
                if idx - 2 in wdone:
                    p.wait("sync", "tensor", wdone[idx - 2])
                p.wait("sync", *wwrite_name[name])
                sv = p.op("sync", lambda b, sl=slot, i=WSCR[name]:
                          b.dma_start(out=sl[:, :], in_=i), sig=16, sem=dr)
                p.wait("tensor", dr, sv)
            return idx, slot

        def w_done(idx):
            wdone[idx] = p.cnt["tensor"]

        def bcast_dma(dst, src1d):
            a = src1d.rearrange("(o f) -> o f", o=1).partition_broadcast(128)
            return p.op("sync", lambda b, dd=dst, i=a: b.dma_start(out=dd[:, :], in_=i), sig=16)

        bcast_dma(BClng, ln_g)
        bcast_dma(BClnb, ln_b)
        bcast_dma(BCnw[0], norm_w[0])
        bcast_dma(BCnw[1], norm_w[1])
        bcast_dma(BCbproj, b_proj)
        bcast_dma(BCbv, b_qkv[2 * D:3 * D])
        bcast_dma(BCsbd2, share_bdown)
        p.op("sync", lambda b: b.dma_start(
            out=RB[:, :], in_=router_b.rearrange("j e -> (j e)").rearrange("(o f) -> o f", o=1).partition_broadcast(128)), sig=16)
        p.op("sync", lambda b: b.dma_start(
            out=GB[:, :], in_=gate_bias.rearrange("j e -> (j e)").rearrange("(o f) -> o f", o=1).partition_broadcast(128)), sig=16)
        p.op("sync", lambda b: b.dma_start(out=BQKV[:, 0:8], in_=b_qkv[0:D].rearrange("(c p) -> p c", p=128), allow_slow_non_contiguous=True), sig=16)
        p.op("sync", lambda b: b.dma_start(out=BQKV[:, 8:16], in_=b_qkv[D:2 * D].rearrange("(c p) -> p c", p=128), allow_slow_non_contiguous=True), sig=16)
        p.op("sync", lambda b: b.dma_start(out=BGATE[:, :], in_=b_gate.rearrange("(c p) -> p c", p=128), allow_slow_non_contiguous=True), sig=16)
        p.op("sync", lambda b: b.dma_start(out=BUP[:, :].rearrange("p (e c) -> p e c", e=NE),
                                           in_=exp_bup.rearrange("e (c p) -> p e c", p=128), allow_slow_non_contiguous=True), sig=16)
        p.op("sync", lambda b: b.dma_start(out=BGE[:, :].rearrange("p (e c) -> p e c", e=NE),
                                           in_=exp_bg.rearrange("e (c p) -> p e c", p=128), allow_slow_non_contiguous=True), sig=16)
        p.op("sync", lambda b: b.dma_start(out=SBUP[:, :].rearrange("p (sl c) -> p sl c", sl=2),
                                           in_=share_bup.rearrange("(sl c p) -> p sl c", p=128, sl=2), allow_slow_non_contiguous=True), sig=16)
        p.op("sync", lambda b: b.dma_start(out=SBG[:, :], in_=share_bg.rearrange("(c p) -> p c", p=128), allow_slow_non_contiguous=True), sig=16)
        p.op("sync", lambda b: b.dma_start(out=BDR[:, :], in_=exp_bdown), sig=16)
        p.op("sync", lambda b: b.dma_start(out=IDB[:, :], in_=ident_b), sig=16)
        p.op("sync", lambda b: b.dma_start(out=IDF[:, :], in_=ident_f), sig=16)
        startup_sync = p.op("sync", lambda b: b.dma_start(out=MASK[:, :], in_=mask_c), sig=16)

        rw_src = router_W.rearrange("j (c p) e -> p j c e", p=128)
        p.op("gpsimd", lambda b: b.dma_start(
            out=RW[:, :].rearrange("p (j c e) -> p j c e", j=2, c=KC), in_=rw_src), sig=16, sem="dmi")
        rw_sig = p.op("gpsimd", lambda b: b.dma_start(out=BDRB[:, :], in_=exp_bdown), sig=16, sem="dmi")
        x4 = x_d.rearrange("(tl sbt pp) d -> tl pp sbt d", pp=128, sbt=NSUB)
        out4 = out_d.rearrange("(tl sbt pp) d -> tl pp sbt d", pp=128, sbt=NSUB)
        xload_sig = {0: p.op("gpsimd", lambda b: b.dma_start(
            out=X[0][:, :].rearrange("p (sbt d) -> p sbt d", sbt=NSUB), in_=x4[0]), sig=16, sem="dxi")}
        p.wait("tensor", "dmi", rw_sig)
        p.wait("vector", "dmi", rw_sig)
        p.wait("scalar", "dmi", rw_sig)

        p.wait("vector", "sync", startup_sync)
        p.op("vector", lambda b: b.tensor_scalar(out=BCsbd2[:, :], in0=BCsbd2[:, :],
                                                 scalar1=2.0, scalar2=None, op0=AL.mult))
        p.op("vector", lambda b: b.memset(S[:, :], 0.0))
        p.op("vector", lambda b: b.memset(KS[:, :], 0.0))
        p.op("vector", lambda b: b.memset(KSB[:, :], 0.0))
        p.op("vector", lambda b: b.memset(ONES1[:, :], 1.0))
        p.op("vector", lambda b: b.memset(CONSTS[:, 0:1], LN_EPS))
        p.op("vector", lambda b: b.memset(CONSTS[:, 1:2], RMS_EPS))
        p.op("vector", lambda b: b.memset(G8[:, :], -BIG))
        memv = p.op("vector", lambda b: b.memset(BIAS8[:, :], -BIG), sig=1)
        p.wait("vector", "sync", startup_sync)
        p.wait("scalar", "sync", startup_sync)
        p.wait("tensor", "sync", startup_sync)
        p.wait("tensor", "vector", memv)
        p.wait("scalar", "vector", memv)

        C_NMU, C_NMUD, C_SQ, C_RSTD = 0, 4, 8, 12
        C_DEN, C_REC, C_NM, C_SSUM, C_SREC = 16, 20, 24, 28, 32
        C_KSR = 36

        def sub_ap(buf, sub):
            return buf[:, sub * D:(sub + 1) * D]

        def emit_ln(src_aps, dst_aps, affine):
            for sub in range(NSUB):
                v(lambda b, a=src_aps[sub], o=SCR[:, C_NMU + sub:C_NMU + sub + 1]:
                  b.reduce_sum(o, a, axis=AX.X, negate=True))
            nmd = 0
            for sub in range(NSUB):
                nmd = v(lambda b, i=SCR[:, C_NMU + sub:C_NMU + sub + 1],
                        o=SCR[:, C_NMUD + sub:C_NMUD + sub + 1]:
                        b.tensor_scalar(out=o, in0=i, scalar1=1.0 / D, scalar2=None, op0=AL.mult),
                        sig=(1 if sub == NSUB - 1 else 0))
            p.wait("scalar", "vector", nmd)
            for sub in range(NSUB):
                s(lambda b, a=src_aps[sub], m=SCR[:, C_NMUD + sub:C_NMUD + sub + 1],
                  acc=SCR[:, C_SQ + sub:C_SQ + sub + 1]:
                  b.activation(out=GS[:, 0:D], in_=a, func=AF.Square, bias=m, accum_out=acc))
            for sub in range(NSUB):
                s(lambda b, i=SCR[:, C_SQ + sub:C_SQ + sub + 1]:
                  b.activation(out=i, in_=i, func=AF.Ln, scale=1.0 / D, bias=CONSTS[:, 0:1]))
            rs = 0
            for sub in range(NSUB):
                rs = s(lambda b, i=SCR[:, C_SQ + sub:C_SQ + sub + 1],
                       o=SCR[:, C_RSTD + sub:C_RSTD + sub + 1]:
                       b.activation(out=o, in_=i, func=AF.Exp, scale=-0.5),
                       sig=(1 if sub == NSUB - 1 else 0))
            p.wait("vector", "scalar", rs)
            out_sig = 0
            for sub in range(NSUB):
                out_sig = v(lambda b, a=src_aps[sub], o=dst_aps[sub],
                            m=SCR[:, C_NMUD + sub:C_NMUD + sub + 1],
                            r=SCR[:, C_RSTD + sub:C_RSTD + sub + 1]:
                            b.tensor_scalar(out=o, in0=a, scalar1=m, scalar2=r,
                                            op0=AL.add, op1=AL.mult),
                            sig=(0 if affine else (1 if sub == NSUB - 1 else 0)))
            if affine:
                for sub in range(NSUB):
                    v(lambda b, a=dst_aps[sub]: b.tensor_tensor(a, a, BClng[:, :], AL.mult))
                for sub in range(NSUB):
                    out_sig = v(lambda b, a=dst_aps[sub]:
                                b.tensor_tensor(a, a, BClnb[:, :], AL.add),
                                sig=(1 if sub == NSUB - 1 else 0))
            return out_sig

        b1_guard = [0]
        b2_guard = [0]

        def emit_to_fm(src_aps, dst_fm, v_ready_sig):
            p.wait("scalar", "vector", v_ready_sig)
            p.wait("scalar", "tensor", b2_guard[0])
            cast_sig = 0
            for sub in range(NSUB):
                cast_sig = s(lambda b, a=src_aps[sub], o=B2[:, sub * D:(sub + 1) * D]:
                             b.copy(o, a), sig=(1 if sub == NSUB - 1 else 0))
            p.wait("tensor", "scalar", cast_sig)
            p.wait("scalar", "tensor", b1_guard[0])
            ev = 0
            for c in range(KC):
                ptr = reg_start("PTR")
                for j in range(NSUB):
                    t(lambda b, pa=ptr[:, j * 128:(j + 1) * 128],
                      ia=B2[:, j * D + c * 128: j * D + (c + 1) * 128]:
                      b.transpose(pa, ia, IDB[:, :]), sig=(1 if j == NSUB - 1 else 0))
                p.wait("scalar", "tensor", p.cnt["tensor"])
                ev = s(lambda b, pa=ptr[:, 0:512], o=dst_fm[:, c * 512:(c + 1) * 512]:
                       b.copy(o, pa), sig=1)
                reg_evac_done("PTR", "scalar")
            return ev

        def fm_group(psname, slot, o, rhs_fm):
            ps = reg_start(psname)
            sig = 0
            for c in range(KC):
                sig = t(lambda b, pa=ps[:, 0:512],
                        l=slot[:, c * D + o * 128: c * D + o * 128 + 128],
                        r=rhs_fm[:, c * 512: (c + 1) * 512],
                        st=(c == 0), sp=(c == KC - 1):
                        b.matmul(pa, l, r, start=st, stop=sp),
                        sig=(1 if c == KC - 1 else 0))
            return sig, ps

        PFROT = ["PF0", "PF1", "PF2"]

        B4f = B4[:, :].bitcast(F32)
        B5f = B5[:, :].bitcast(F32)
        SCRX = [B4f[:, 0:D], B4f[:, D:2 * D], B5f[:, 0:D], B5f[:, D:2 * D]]
        pre_xnfm = {}
        pre_pw6 = {}

        def emit_preattn(ti):
            p.wait("vector", "dxi", xload_sig[ti])
            p.wait("scalar", "dxi", xload_sig[ti])
            p.wait("tensor", "dxi", xload_sig[ti])
            p.wait("vector", "tensor", b2_guard[0])
            src = [sub_ap(X[ti % 2], sub) for sub in range(NSUB)]
            emit_ln(src, SCRX, affine=False)
            xn_sig = emit_ln(SCRX, SCRX, affine=True)
            pre_xnfm[ti] = emit_to_fm(SCRX, B1, xn_sig)
            pre_pw6[ti] = p.cnt["scalar"]
            return pre_xnfm[ti]

        outdma_sig = {}
        fin_sig = [0]
        state_upd_sig = [0]
        ksb_write_sig = [0]
        gs_free = [0, 0]

        try:
         for ti in range(nt):
             cur = X[ti % 2]
             oth = X[(ti + 1) % 2]
             SB16 = oth[:, :].bitcast(BF16)
             first = (ti == 0)

             if first:
                 emit_preattn(0)
             xnfm_sig = pre_xnfm[ti]
             tap(f"xnfm{ti}", B1[:, :], "scalar")

             if not first:
                 p.wait("scalar", "vector", state_upd_sig[0])
                 p.wait("scalar", "vector", fin_sig[0])
                 for c in range(KC):
                     s(lambda b, a=S[:, c * D:(c + 1) * D], o=SB16[:, c * D:(c + 1) * D]:
                       b.copy(o, a), sig=(1 if c == KC - 1 else 0))
             scast_sig = p.cnt["scalar"]

             p.wait("tensor", "scalar", xnfm_sig)
             tap(f"cp1_{ti}", SCR[:, 0:4], "vector")
             idx, slot = w_use("wgate")
             tap(f"wgslot{ti}", W0[:, :], "gpsimd")
             for o in range(KC):
                 sig, ps = fm_group(PFROT[o % 3], slot, o, B1)
                 p.wait("scalar", "tensor", sig)
                 s(lambda b, pa=ps[:, 0:512], bb=BGATE[:, o:o + 1], oo=B5[:, o * 512:(o + 1) * 512]:
                   b.activation(out=oo, in_=pa, func=AF.Sigmoid, bias=bb), sig=1)
                 reg_evac_done(PFROT[o % 3], "scalar")
             w_done(idx)
             sig_sigma = p.cnt["scalar"]
             tap(f"sig{ti}", B5[:, :], "scalar")

             idx, slot = w_use("wq")
             for o in range(KC):
                 sig, ps = fm_group(PFROT[o % 3], slot, o, B1)
                 p.wait("vector", "tensor", sig)
                 p.wait("vector", "tensor", b2_guard[0])
                 v(lambda b, pa=ps[:, 0:512], bb=BQKV[:, o:o + 1]:
                   b.tensor_scalar(out=EL1[:, :], in0=pa, scalar1=bb, scalar2=0.0,
                                   op0=AL.add, op1=AL.min))
                 vv = v(lambda b, pa=ps[:, 0:512], bb=BQKV[:, o:o + 1]:
                        b.tensor_scalar(out=EL2[:, :], in0=pa, scalar1=bb, scalar2=0.0,
                                        op0=AL.add, op1=AL.max), sig=1)
                 reg_evac_done(PFROT[o % 3], "vector")
                 p.wait("scalar", "vector", vv)
                 ss = s(lambda b: b.activation(out=EL1[:, :], in_=EL1[:, :], func=AF.Exp), sig=1)
                 p.wait("vector", "scalar", ss)
                 v(lambda b, oo=B2[:, o * 512:(o + 1) * 512]:
                   b.tensor_tensor(oo, EL1[:, :], EL2[:, :], AL.add), sig=1)
             w_done(idx)
             qf_sig = p.cnt["vector"]
             tap(f"qf{ti}", B2[:, :], "vector")

             idx, slot = w_use("wk")
             p.wait("vector", "scalar", sig_sigma)
             for o in range(KC):
                 sig, ps = fm_group(PFROT[o % 3], slot, o, B1)
                 p.wait("vector", "tensor", sig)
                 v(lambda b, pa=ps[:, 0:512], bb=BQKV[:, 8 + o:9 + o], sg=B5[:, o * 512:(o + 1) * 512]:
                   b.scalar_tensor_tensor(out=EL1[:, :], in0=pa, scalar=bb, in1=sg,
                                          op0=AL.add, op1=AL.mult))
                 v(lambda b: b.tensor_scalar(out=EL2[:, :], in0=EL1[:, :], scalar1=0.0,
                                             scalar2=None, op0=AL.min))
                 vv = v(lambda b: b.tensor_scalar(out=EL1[:, :], in0=EL1[:, :], scalar1=0.0,
                                                  scalar2=None, op0=AL.max), sig=1)
                 reg_evac_done(PFROT[o % 3], "vector")
                 p.wait("scalar", "vector", vv)
                 ss = s(lambda b: b.activation(out=EL2[:, :], in_=EL2[:, :], func=AF.Exp), sig=1)
                 p.wait("vector", "scalar", ss)
                 v(lambda b, oo=B3[:, o * 512:(o + 1) * 512]:
                   b.tensor_tensor(oo, EL2[:, :], EL1[:, :], AL.add), sig=1)
             w_done(idx)
             kf_sig = p.cnt["vector"]
             tap(f"kf{ti}", B3[:, :], "vector")

             idx, slot = w_use("wv")
             for j in range(NSUB):
                 psn = "PT0" if j % 2 == 0 else "PT1"
                 ps = reg_start(psn)
                 sig = 0
                 for h in range(2):
                     for c in range(KC):
                         sig = t(lambda b, pa=ps[:, h * 512:(h + 1) * 512],
                                 l=B1[:, c * 512 + j * 128: c * 512 + j * 128 + 128],
                                 r=slot[:, c * D + h * 512: c * D + h * 512 + 512],
                                 st=(c == 0), sp=(c == KC - 1):
                                 b.matmul(pa, l, r, start=st, stop=sp),
                                 sig=(1 if h == 1 and c == KC - 1 else 0))
                 p.wait("vector", "tensor", sig)
                 v(lambda b, pa=ps[:, 0:1024], oo=B4[:, j * D:(j + 1) * D]:
                   b.tensor_tensor(oo, pa, BCbv[:, :], AL.add), sig=1)
                 reg_evac_done(psn, "vector")
             w_done(idx)
             vproj_done = p.cnt["tensor"]
             b1_guard[0] = vproj_done
             vt_sig = p.cnt["vector"]
             tap(f"vt{ti}", B4[:, :], "vector")

             p.wait("tensor", "vector", kf_sig)
             for cs in range(NSUB):
                 psn = PFROT[cs % 3]
                 ps = reg_start(psn)
                 sig = 0
                 for c in range(KC):
                     sig = t(lambda b, pa=ps[:, 0:512],
                             l=B3[:, c * 512 + cs * 128: c * 512 + cs * 128 + 128],
                             r=B2[:, c * 512: (c + 1) * 512],
                             st=(c == 0), sp=(c == KC - 1):
                             b.matmul(pa, l, r, start=st, stop=sp),
                             sig=(1 if c == KC - 1 else 0))
                 p.wait("vector", "tensor", sig)
                 v(lambda b, pa=ps[:, 0:512], mk=MASK[:, 384 - 128 * cs: 384 - 128 * cs + 512],
                   oo=SC[:, cs * 512:(cs + 1) * 512]:
                   b.tensor_tensor(oo, pa, mk, AL.mult), sig=1)
                 reg_evac_done(psn, "vector")
             sc_sig = p.cnt["vector"]
             tap(f"sc{ti}", SC[:, :], "vector")

             p.wait("scalar", "vector", kf_sig)
             kt_sig = 0
             for cs in range(NSUB):
                 for half in range(2):
                     ptr = reg_start("PTR")
                     for jj in range(4):
                         c = half * 4 + jj
                         t(lambda b, pa=ptr[:, jj * 128:(jj + 1) * 128],
                           ia=B3[:, c * 512 + cs * 128: c * 512 + (cs) * 128 + 128]:
                           b.transpose(pa, ia, IDB[:, :]), sig=(1 if jj == 3 else 0))
                     p.wait("scalar", "tensor", p.cnt["tensor"])
                     kt_sig = s(lambda b, pa=ptr[:, 0:512],
                                o=B5[:, cs * D + half * 512: cs * D + half * 512 + 512]:
                                b.copy(o, pa), sig=1)
                     reg_evac_done("PTR", "scalar")
             tap(f"kt{ti}", B5[:, :], "scalar")

             p.wait("tensor", "vector", sc_sig)
             p.wait("tensor", "vector", vt_sig)
             if not first:
                 p.wait("tensor", "scalar", scast_sig)
             num_evac = 0
             for o in range(KC):
                 psn = PFROT[o % 3]
                 ps = reg_start(psn)
                 nmm = NSUB + (0 if first else KC)
                 k_i = 0
                 sig = 0
                 for cs in range(NSUB):
                     sig = t(lambda b, pa=ps[:, 0:512],
                             l=B4[:, cs * D + o * 128: cs * D + o * 128 + 128],
                             r=SC[:, cs * 512:(cs + 1) * 512],
                             st=(k_i == 0), sp=(k_i == nmm - 1):
                             b.matmul(pa, l, r, start=st, stop=sp),
                             sig=(1 if k_i == nmm - 1 else 0))
                     k_i += 1
                 if not first:
                     for c in range(KC):
                         sig = t(lambda b, pa=ps[:, 0:512],
                                 l=SB16[:, c * D + o * 128: c * D + o * 128 + 128],
                                 r=B2[:, c * 512:(c + 1) * 512],
                                 st=(k_i == 0), sp=(k_i == nmm - 1):
                                 b.matmul(pa, l, r, start=st, stop=sp),
                                 sig=(1 if k_i == nmm - 1 else 0))
                         k_i += 1
                 p.wait("scalar", "tensor", sig)
                 num_evac = s(lambda b, pa=ps[:, 0:512], o=B1[:, o * 512:(o + 1) * 512]:
                              b.copy(o, pa), sig=1)
                 reg_evac_done(psn, "scalar")
             tap(f"num{ti}", B1[:, :], "scalar")

             tap(f"ksb{ti}", KSB[:, :], "scalar")
             p.wait("tensor", "scalar", kt_sig)
             if not first:
                 p.wait("tensor", "scalar", ksb_write_sig[0])
             psd = reg_start("PDEN")
             den_mm = 0
             for j in range(NSUB):
                 nmm = (j + 1) + (0 if first else KC)
                 k_i = 0
                 last_overall = (j == NSUB - 1)
                 for cs in range(j + 1):
                     den_mm = t(lambda b, pa=psd[:, j:j + 1],
                                l=SC[:, cs * 512 + j * 128: cs * 512 + j * 128 + 128],
                                r=ONES1[:, 0:1],
                                st=(k_i == 0), sp=(k_i == nmm - 1):
                                b.matmul(pa, l, r, start=st, stop=sp),
                                sig=(1 if last_overall and k_i == nmm - 1 else 0))
                     k_i += 1
                 if not first:
                     for c in range(KC):
                         den_mm = t(lambda b, pa=psd[:, j:j + 1],
                                    l=B2[:, c * 512 + j * 128: c * 512 + j * 128 + 128],
                                    r=KSB[:, c:c + 1],
                                    st=(k_i == 0), sp=(k_i == nmm - 1):
                                    b.matmul(pa, l, r, start=st, stop=sp),
                                    sig=(1 if last_overall and k_i == nmm - 1 else 0))
                         k_i += 1
             b2_guard[0] = den_mm
             p.wait("vector", "tensor", den_mm)
             v(lambda b: b.tensor_copy(SCR[:, 60:64], SCR[:, 0:4]))
             v(lambda b: b.tensor_copy(SCR[:, 60:64], SCR[:, 0:4]))
             den_ts = v(lambda b: b.tensor_scalar(out=SCR[:, C_DEN:C_DEN + 4], in0=psd[:, 0:4],
                                                  scalar1=DEN_EPS, scalar2=None, op0=AL.add), sig=1)
             reg_evac_done("PDEN", "vector")
             p.wait("scalar", "vector", den_ts)
             s(lambda b: b.activation(out=SCR[:, C_REC:C_REC + 4], in_=SCR[:, C_DEN:C_DEN + 4],
                                      func=AF.Ln))
             den_ev = s(lambda b: b.activation(out=SCR[:, C_REC:C_REC + 4], in_=SCR[:, C_REC:C_REC + 4],
                                               func=AF.Exp, scale=-1.0), sig=1)
             tap(f"rec{ti}", SCR[:, C_DEN:C_DEN + 8], "scalar")
             tap_den = f"den{ti}"
             if tap_den in taps:
                 tap(tap_den, SCR[:, C_DEN:C_DEN + 4], "vector")

             p.wait("vector", "vector", 0)
             for c in range(KC):
                 v(lambda b, a=B3[:, c * 512:(c + 1) * 512], o=SCR[:, C_KSR + c:C_KSR + c + 1]:
                   b.reduce_sum(o, a, axis=AX.X))
             p.wait("tensor", "scalar", kt_sig)
             for o in range(KC):
                 psn = "PT0" if o % 2 == 0 else "PT1"
                 ps = reg_start(psn)
                 sig = 0
                 for h in range(2):
                     for cs in range(NSUB):
                         sig = t(lambda b, pa=ps[:, h * 512:(h + 1) * 512],
                                 l=B5[:, cs * D + o * 128: cs * D + o * 128 + 128],
                                 r=B4[:, cs * D + h * 512: cs * D + h * 512 + 512],
                                 st=(cs == 0), sp=(cs == NSUB - 1):
                                 b.matmul(pa, l, r, start=st, stop=sp),
                                 sig=(1 if h == 1 and cs == NSUB - 1 else 0))
                 p.wait("vector", "tensor", sig)
                 if not first:
                     p.wait("vector", "scalar", scast_sig)
                 v(lambda b, pa=ps[:, 0:1024], ss_=S[:, o * D:(o + 1) * D]:
                   b.tensor_tensor(ss_, ss_, pa, AL.add), sig=1)
                 reg_evac_done(psn, "vector")
             state_mm_done = p.cnt["tensor"]
             state_upd_sig[0] = v(lambda b: b.tensor_tensor(
                 KS[:, 0:KC], KS[:, 0:KC], SCR[:, C_KSR:C_KSR + KC], AL.add), sig=1)
             p.wait("scalar", "vector", state_upd_sig[0])
             p.wait("scalar", "tensor", den_mm)
             ksb_write_sig[0] = s(lambda b: b.copy(KSB[:, :], KS[:, :]), sig=1)
             if f"S{ti}" in taps:
                 tap(f"S{ti}", S[:, :], "vector")

             idx, slot = w_use("wproj")
             p.wait("tensor", "scalar", num_evac)
             for j in range(NSUB):
                 psn = "PT0" if j % 2 == 0 else "PT1"
                 ps = reg_start(psn)
                 sig = 0
                 for h in range(2):
                     for c in range(KC):
                         sig = t(lambda b, pa=ps[:, h * 512:(h + 1) * 512],
                                 l=B1[:, c * 512 + j * 128: c * 512 + j * 128 + 128],
                                 r=slot[:, c * D + h * 512: c * D + h * 512 + 512],
                                 st=(c == 0), sp=(c == KC - 1):
                                 b.matmul(pa, l, r, start=st, stop=sp),
                                 sig=(1 if h == 1 and c == KC - 1 else 0))
                 p.wait("vector", "tensor", sig)
                 p.wait("vector", "scalar", den_ev)
                 v(lambda b, pa=ps[:, 0:1024], xs=sub_ap(cur, j), rc=SCR[:, C_REC + j:C_REC + j + 1]:
                   b.scalar_tensor_tensor(out=xs, in0=pa, scalar=rc, in1=xs,
                                          op0=AL.mult, op1=AL.add))
                 v(lambda b, xs=sub_ap(cur, j): b.tensor_tensor(xs, xs, BCbproj[:, :], AL.add), sig=1)
                 reg_evac_done(psn, "vector")
             w_done(idx)
             proj_mm_done = p.cnt["tensor"]
             b1_guard[0] = proj_mm_done
             xp_sig = p.cnt["vector"]
             tap(f"xp{ti}", cur[:, :], "vector")

             if ti + 1 < nt:
                 p.wait("gpsimd", "tensor", state_mm_done)
                 p.wait("gpsimd", "scalar", scast_sig if not first else xnfm_sig)
                 xload_sig[ti + 1] = p.op("gpsimd", lambda b, dd=oth, i=x4[ti + 1]: b.dma_start(
                     out=dd[:, :].rearrange("p (sbt d) -> p sbt d", sbt=NSUB), in_=i), sig=16, sem="dxi")

             if ti > 0:
                 p.wait("vector", "dxo", outdma_sig[ti - 1])
             ln_sig = emit_ln([sub_ap(cur, sub) for sub in range(NSUB)],
                              [sub_ap(XF, sub) for sub in range(NSUB)], affine=False)
             tap(f"xf{ti}", XF[:, :], "vector")

             for jc in range(2):
                 xffm_sig = emit_to_fm([sub_ap(XF, sub) for sub in range(NSUB)], B1, ln_sig)
                 tr_done = p.cnt["tensor"]

                 p.wait("tensor", "scalar", xffm_sig)
                 psr = reg_start("PRT")
                 rt_mm = 0
                 for j in range(NSUB):
                     for c in range(KC):
                         rt_mm = t(lambda b, pa=psr[:, j * 6:(j + 1) * 6],
                                   l=B1[:, c * 512 + j * 128: c * 512 + j * 128 + 128],
                                   r=RW[:, jc * KC * NE + c * NE: jc * KC * NE + (c + 1) * NE],
                                   st=(c == 0), sp=(c == KC - 1):
                                   b.matmul(pa, l, r, start=st, stop=sp),
                                   sig=(1 if j == NSUB - 1 and c == KC - 1 else 0))
                 p.wait("vector", "tensor", rt_mm)
                 rt_ev = 0
                 for j in range(NSUB):
                     rt_ev = v(lambda b, pa=psr[:, j * 6:(j + 1) * 6], o=G8[:, j * 8:j * 8 + 6],
                               rb=RB[:, jc * 6:(jc + 1) * 6]:
                               b.tensor_tensor(o, pa, rb, AL.add),
                               sig=(1 if j == NSUB - 1 else 0))
                 reg_evac_done("PRT", "vector")
                 for j in range(NSUB):
                     v(lambda b, i=G8[:, j * 8:j * 8 + 6], o=BIAS8[:, j * 8:j * 8 + 6],
                       gb=GB[:, jc * 6:(jc + 1) * 6]:
                       b.tensor_tensor(o, i, gb, AL.add))
                 for j in range(NSUB):
                     v(lambda b, i=BIAS8[:, j * 8:(j + 1) * 8], o=M8[:, j * 8:(j + 1) * 8]:
                       b.max(out=o, in_=i))
                 for j in range(NSUB):
                     v(lambda b, i=BIAS8[:, j * 8:j * 8 + 6], o=SEL[:, j * 6:(j + 1) * 6],
                       m2=M8[:, j * 8 + 1:j * 8 + 2]:
                       b.tensor_single_scalar(o, i, m2, AL.is_ge))
                 for j in range(NSUB):
                     v(lambda b, a=SEL[:, j * 6:(j + 1) * 6]:
                       b.tensor_scalar(out=a, in0=a, scalar1=1.0, scalar2=BIG,
                                       op0=AL.subtract, op1=AL.mult))
                 for j in range(NSUB):
                     v(lambda b, ga=G8[:, j * 8:j * 8 + 6], se=SEL[:, j * 6:(j + 1) * 6],
                       o=GSEL[:, j * 6:(j + 1) * 6]:
                       b.tensor_tensor(o, ga, se, AL.add))
                 for j in range(NSUB):
                     v(lambda b, i=GSEL[:, j * 6:(j + 1) * 6], o=SCR[:, C_NM + j:C_NM + j + 1]:
                       b.reduce_max(o, i, axis=AX.X, negate=True))
                 v(lambda b: b.tensor_copy(SCR[:, 60:64], SCR[:, C_NM:C_NM + 4]))
                 nm = v(lambda b: b.tensor_copy(SCR[:, 60:64], SCR[:, C_NM:C_NM + 4]), sig=1)
                 p.wait("scalar", "vector", nm)
                 es = 0
                 for j in range(NSUB):
                     es = s(lambda b, a=GSEL[:, j * 6:(j + 1) * 6], bi=SCR[:, C_NM + j:C_NM + j + 1]:
                            b.activation(out=a, in_=a, func=AF.Exp, bias=bi),
                            sig=(1 if j == NSUB - 1 else 0))
                 p.wait("vector", "scalar", es)
                 for j in range(NSUB):
                     v(lambda b, i=GSEL[:, j * 6:(j + 1) * 6], o=SCR[:, C_SSUM + j:C_SSUM + j + 1]:
                       b.reduce_sum(o, i, axis=AX.X))
                 v(lambda b: b.tensor_copy(SCR[:, 60:64], SCR[:, C_SSUM:C_SSUM + 4]))
                 ssum_sig = v(lambda b: b.tensor_copy(SCR[:, 60:64], SCR[:, C_SSUM:C_SSUM + 4]), sig=1)
                 p.wait("scalar", "vector", ssum_sig)
                 s(lambda b: b.activation(out=SCR[:, C_SREC:C_SREC + 4], in_=SCR[:, C_SSUM:C_SSUM + 4],
                                          func=AF.Ln))
                 srec_sig = s(lambda b: b.activation(out=SCR[:, C_SREC:C_SREC + 4], in_=SCR[:, C_SREC:C_SREC + 4],
                                                     func=AF.Exp, scale=-1.0), sig=1)
                 p.wait("vector", "scalar", srec_sig)
                 w6s = 0
                 for j in range(NSUB):
                     w6s = v(lambda b, i=GSEL[:, j * 6:(j + 1) * 6], o=W6[:, j * 6:(j + 1) * 6],
                             rc=SCR[:, C_SREC + j:C_SREC + j + 1]:
                             b.tensor_scalar(out=o, in0=i, scalar1=rc, scalar2=None, op0=AL.mult),
                             sig=(1 if j == NSUB - 1 else 0))
                 if f"w6_{ti}_{jc}" in taps:
                     tap(f"w6_{ti}_{jc}", W6[:, :], "vector")

                 for e in range(NE):
                     iu, slu = w_use(f"up{e}")
                     p.wait("scalar", "tensor", b2_guard[0])
                     h_ev = 0
                     for o in range(KC):
                         sig, ps = fm_group(PFROT[o % 3], slu, o, B1)
                         p.wait("scalar", "tensor", sig)
                         h_ev = s(lambda b, pa=ps[:, 0:512], bb=BUP[:, e * KC + o:e * KC + o + 1],
                                  oo=B2[:, o * 512:(o + 1) * 512]:
                                  b.activation(out=oo, in_=pa, func=AF.Identity, bias=bb), sig=1)
                         reg_evac_done(PFROT[o % 3], "scalar")
                     w_done(iu)
                     ig, slg = w_use(f"wg{e}")
                     p.wait("tensor", "scalar", h_ev)
                     prod_sig = 0
                     for o in range(KC):
                         sig, ps = fm_group(PFROT[o % 3], slg, o, B2)
                         p.wait("scalar", "tensor", sig)
                         p.wait("scalar", "vector", gs_free[o % 2])
                         gsig = s(lambda b, pa=ps[:, 0:512], bb=BGE[:, e * KC + o:e * KC + o + 1],
                                  oo=GS[:, (o % 2) * 512:(o % 2) * 512 + 512]:
                                  b.activation(out=oo, in_=pa, func=AF.Silu, bias=bb), sig=1)
                         reg_evac_done(PFROT[o % 3], "scalar")
                         p.wait("vector", "scalar", gsig)
                         prod_sig = v(lambda b, gg=GS[:, (o % 2) * 512:(o % 2) * 512 + 512],
                                      hh=B2[:, o * 512:(o + 1) * 512],
                                      oo=B3[:, o * 512:(o + 1) * 512]:
                                      b.tensor_tensor(oo, gg, hh, AL.mult), sig=1)
                         gs_free[o % 2] = prod_sig
                     w_done(ig)
                     idn, sld = w_use(f"dn{e}")
                     p.wait("tensor", "vector", prod_sig)
                     for j in range(NSUB):
                         psn = "PT0" if j % 2 == 0 else "PT1"
                         ps = reg_start(psn)
                         sig = 0
                         for h in range(2):
                             for c in range(KC):
                                 sig = t(lambda b, pa=ps[:, h * 512:(h + 1) * 512],
                                         l=B3[:, c * 512 + j * 128: c * 512 + j * 128 + 128],
                                         r=sld[:, c * D + h * 512: c * D + h * 512 + 512],
                                         st=(c == 0), sp=(c == KC - 1):
                                         b.matmul(pa, l, r, start=st, stop=sp),
                                         sig=(1 if h == 1 and c == KC - 1 else 0))
                         p.wait("vector", "tensor", sig)
                         v(lambda b, pa=ps[:, 0:1024], xs=sub_ap(XF, j),
                           ww=W6[:, j * 6 + e:j * 6 + e + 1]:
                           b.scalar_tensor_tensor(out=xs, in0=pa, scalar=ww, in1=xs,
                                                  op0=AL.mult, op1=AL.add), sig=1)
                         reg_evac_done(psn, "vector")
                     w_done(idn)
                 b2_guard[0] = p.cnt["tensor"]

                 iu0, sl0 = w_use("sup0")
                 p.wait("scalar", "tensor", state_mm_done)
                 h0_ev = 0
                 for o in range(KC):
                     sig, ps = fm_group(PFROT[o % 3], sl0, o, B1)
                     p.wait("scalar", "tensor", sig)
                     h0_ev = s(lambda b, pa=ps[:, 0:512], bb=SBUP[:, o:o + 1],
                               oo=B4[:, o * 512:(o + 1) * 512]:
                               b.activation(out=oo, in_=pa, func=AF.Identity, bias=bb), sig=1)
                     reg_evac_done(PFROT[o % 3], "scalar")
                 w_done(iu0)
                 iu1, sl1 = w_use("sup1")
                 h1_ev = 0
                 for o in range(KC):
                     sig, ps = fm_group(PFROT[o % 3], sl1, o, B1)
                     p.wait("scalar", "tensor", sig)
                     h1_ev = s(lambda b, pa=ps[:, 0:512], bb=SBUP[:, KC + o:KC + o + 1],
                               oo=B5[:, o * 512:(o + 1) * 512]:
                               b.activation(out=oo, in_=pa, func=AF.Identity, bias=bb), sig=1)
                     reg_evac_done(PFROT[o % 3], "scalar")
                 w_done(iu1)
                 b1_guard[0] = p.cnt["tensor"]

                 isg, slsg = w_use("sg")
                 p.wait("tensor", "scalar", h1_ev)
                 prod_sigs = [0, 0]
                 for sl_i, (hbuf, pbuf) in enumerate(((B4, B3), (B5, B2))):
                     for o in range(KC):
                         ps = reg_start(PFROT[o % 3])
                         sig = 0
                         for c in range(KC):
                             sig = t(lambda b, pa=ps[:, 0:512],
                                     l=slsg[:, c * D + o * 128: c * D + o * 128 + 128],
                                     r=hbuf[:, c * 512:(c + 1) * 512],
                                     st=(c == 0), sp=(c == KC - 1):
                                     b.matmul(pa, l, r, start=st, stop=sp),
                                     sig=(1 if c == KC - 1 else 0))
                         p.wait("scalar", "tensor", sig)
                         p.wait("scalar", "vector", gs_free[o % 2])
                         gsig = s(lambda b, pa=ps[:, 0:512], bb=SBG[:, o:o + 1],
                                  oo=GS[:, (o % 2) * 512:(o % 2) * 512 + 512]:
                                  b.activation(out=oo, in_=pa, func=AF.Silu, bias=bb), sig=1)
                         reg_evac_done(PFROT[o % 3], "scalar")
                         p.wait("vector", "scalar", gsig)
                         prod_sigs[sl_i] = v(lambda b, gg=GS[:, (o % 2) * 512:(o % 2) * 512 + 512],
                                             hh=hbuf[:, o * 512:(o + 1) * 512],
                                             oo=pbuf[:, o * 512:(o + 1) * 512]:
                                             b.tensor_tensor(oo, gg, hh, AL.mult), sig=1)
                         gs_free[o % 2] = prod_sigs[sl_i]
                 w_done(isg)

                 isd, slsd = w_use("sdn")
                 p.wait("tensor", "vector", prod_sigs[1])
                 for j in range(NSUB):
                     psn = "PT0" if j % 2 == 0 else "PT1"
                     ps = reg_start(psn)
                     sig = 0
                     for h in range(2):
                         for sl_i, pbuf in enumerate((B3, B2)):
                             for c in range(KC):
                                 sig = t(lambda b, pa=ps[:, h * 512:(h + 1) * 512],
                                         l=pbuf[:, c * 512 + j * 128: c * 512 + j * 128 + 128],
                                         r=slsd[:, c * D + h * 512: c * D + h * 512 + 512],
                                         st=(sl_i == 0 and c == 0), sp=(sl_i == 1 and c == KC - 1):
                                         b.matmul(pa, l, r, start=st, stop=sp),
                                         sig=(1 if h == 1 and sl_i == 1 and c == KC - 1 else 0))
                     p.wait("vector", "tensor", sig)
                     v(lambda b, pa=ps[:, 0:1024], xs=sub_ap(XF, j):
                       b.tensor_tensor(xs, xs, pa, AL.add))
                     v(lambda b, xs=sub_ap(XF, j):
                       b.tensor_tensor(xs, xs, BCsbd2[:, :], AL.add), sig=1)
                     reg_evac_done(psn, "vector")
                 w_done(isd)
                 b2_guard[0] = p.cnt["tensor"]

                 p.wait("tensor", "vector", w6s)
                 for j in range(NSUB):
                     psw = reg_start("PW6")
                     t(lambda b, pa=psw[:, 0:128], ia=W6[:, j * 6:(j + 1) * 6]:
                       b.transpose(pa, ia, IDF[:, :]), sig=1)
                     p.wait("scalar", "tensor", p.cnt["tensor"])
                     wts = s(lambda b, pa=psw[:, 0:128], o=WT[:, j * 128:(j + 1) * 128]:
                             b.copy(o, pa), sig=1)
                     reg_evac_done("PW6", "scalar")
                     psn = "PT0" if j % 2 == 0 else "PT1"
                     ps = reg_start(psn)
                     p.wait("tensor", "scalar", wts)
                     sig = 0
                     for h in range(2):
                         sig = t(lambda b, pa=ps[:, h * 512:(h + 1) * 512],
                                 l=WT[:, j * 128:(j + 1) * 128],
                                 r=BDRB[:, h * 512:(h + 1) * 512]:
                                 b.matmul(pa, l, r, start=True, stop=True),
                                 sig=(1 if h == 1 else 0))
                     p.wait("vector", "tensor", sig)
                     v(lambda b, pa=ps[:, 0:1024], xs=sub_ap(XF, j):
                       b.tensor_tensor(xs, xs, pa, AL.add), sig=1)
                     reg_evac_done(psn, "vector")

                 if jc == 1 and ti + 1 < nt:
                     emit_preattn(ti + 1)

                 acc_sig = p.cnt["vector"]
                 p.wait("scalar", "vector", acc_sig)
                 for sub in range(NSUB):
                     s(lambda b, a=sub_ap(XF, sub), acc=SCR[:, C_SQ + sub:C_SQ + sub + 1]:
                       b.activation(out=GS[:, 0:D], in_=a, func=AF.Square, accum_out=acc))
                 for sub in range(NSUB):
                     s(lambda b, i=SCR[:, C_SQ + sub:C_SQ + sub + 1]:
                       b.activation(out=i, in_=i, func=AF.Ln, scale=1.0 / D, bias=CONSTS[:, 1:2]))
                 rs = 0
                 for sub in range(NSUB):
                     rs = s(lambda b, i=SCR[:, C_SQ + sub:C_SQ + sub + 1],
                            o=SCR[:, C_RSTD + sub:C_RSTD + sub + 1]:
                            b.activation(out=o, in_=i, func=AF.Exp, scale=-0.5),
                            sig=(1 if sub == NSUB - 1 else 0))
                 p.wait("vector", "scalar", rs)
                 ln_sig = 0
                 for sub in range(NSUB):
                     ln_sig = v(lambda b, a=sub_ap(XF, sub), r=SCR[:, C_RSTD + sub:C_RSTD + sub + 1],
                                nw=BCnw[jc][:, :]:
                                b.scalar_tensor_tensor(out=a, in0=a, scalar=r, in1=nw,
                                                       op0=AL.mult, op1=AL.mult),
                                sig=(1 if sub == NSUB - 1 else 0))
                 if f"ffn{ti}_{jc}" in taps:
                     tap(f"ffn{ti}_{jc}", XF[:, :], "vector")

             fin = 0
             for sub in range(NSUB):
                 fin = v(lambda b, a=sub_ap(XF, sub), xx=sub_ap(cur, sub):
                         b.tensor_tensor(a, a, xx, AL.add),
                         sig=(1 if sub == NSUB - 1 else 0))
             fin_sig[0] = fin
             p.wait("gpsimd", "vector", fin)
             outdma_sig[ti] = p.op("gpsimd", lambda b, i=XF, oo=out4[ti]: b.dma_start(
                 out=oo, in_=i[:, :].rearrange("p (sbt d) -> p sbt d", sbt=NSUB)), sig=16, sem="dxo")

        except _StopRecording:
            pass
        p.wait("gpsimd", "dxo", p.cnt["dxo"])
        p.wait("gpsimd", "dmi", p.cnt["dmi"])

        with nc.Block() as block:
            def runner(eng_name):
                def run(b):
                    for item in p.q[eng_name]:
                        if item[0] == "w":
                            b.wait_ge(sems[item[1]], item[2])
                        else:
                            inst = item[1](b)
                            if item[2]:
                                inst.then_inc(sems[item[3]], item[2])
                return run
            block.tensor(runner("tensor"))
            block.vector(runner("vector"))
            block.scalar(runner("scalar"))
            block.sync(runner("sync"))
            block.gpsimd(runner("gpsimd"))

    return nc, tap_out


def kernel(**inputs):
    nc, _ = build()
    base = {
        "w_qkv": inputs["L0_attn_w_qkv"], "b_qkv": inputs["L0_attn_b_qkv"],
        "w_gate": inputs["L0_attn_w_gate"], "b_gate": inputs["L0_attn_b_gate"],
        "w_proj": inputs["L0_attn_w_proj"], "b_proj": inputs["L0_attn_b_proj"],
        "ln_g": inputs["L0_attn_ln_g"], "ln_b": inputs["L0_attn_ln_b"],
        "router_W": inputs["L0_ffn_router_W"], "router_b": inputs["L0_ffn_router_b"],
        "gate_bias": inputs["L0_ffn_gate_bias"],
        "exp_Wup": inputs["L0_ffn_exp_Wup"], "exp_bup": inputs["L0_ffn_exp_bup"],
        "exp_Wg": inputs["L0_ffn_exp_Wg"], "exp_bg": inputs["L0_ffn_exp_bg"],
        "exp_Wdown": inputs["L0_ffn_exp_Wdown"], "exp_bdown": inputs["L0_ffn_exp_bdown"],
        "share_Wup": inputs["L0_ffn_share_Wup"], "share_bup": inputs["L0_ffn_share_bup"],
        "share_Wg": inputs["L0_ffn_share_Wg"], "share_bg": inputs["L0_ffn_share_bg"],
        "share_Wdown": inputs["L0_ffn_share_Wdown"], "share_bdown": inputs["L0_ffn_share_bdown"],
        "norm_w": inputs["L0_ffn_norm_w"],
    }
    base = {k: np.ascontiguousarray(v, dtype=np.float32) for k, v in base.items()}
    xfull = np.ascontiguousarray(inputs["L0_attn_x"], dtype=np.float32)
    in_maps = []
    for c in range(NCORES):
        m = dict(base)
        m["x"] = np.ascontiguousarray(xfull[c])
        in_maps.append(m)
    res = run_bass_kernel_spmd(nc, in_maps, core_ids=list(range(NCORES)))
    return np.stack([r["out"] for r in res.results], axis=0)


# version 06
# speedup vs baseline: 1.1555x; 1.1555x over previous
import contextlib

import numpy as np
import ml_dtypes
import concourse.bass as bass
import concourse.mybir as mybir
from concourse.bass_utils import run_bass_kernel_spmd

F32 = mybir.dt.float32
BF16 = mybir.dt.bfloat16
AL = mybir.AluOpType
AF = mybir.ActivationFunctionType
AX = mybir.AxisListType

NCORES = 8
T = 4096
D = 1024
TILE = 512
NT = T // TILE
NSUB = TILE // 128
KC = D // 128
NE = 6
LN_EPS = 1e-5
DEN_EPS = 1e-6
RMS_EPS = 1e-6
BIG = 3.0e38

ENGS = ("tensor", "vector", "scalar", "sync", "gpsimd")
DSEMS = ("dwa", "dwb", "dsa", "dsb", "dxi", "dxo", "dmi")
ALLSEMS = ENGS + DSEMS


class Prog:

    def __init__(self):
        self.q = {e: [] for e in ENGS}
        self.cnt = {e: 0 for e in ALLSEMS}

    def wait(self, eng, src, val):
        if val <= 0:
            return
        assert val <= self.cnt[src], f"forward wait {eng}<-{src} {val}>{self.cnt[src]}"
        self.q[eng].append(("w", src, val))

    def op(self, eng, fn, sig=0, sem=None):
        sem = sem or eng
        self.q[eng].append(("o", fn, sig, sem))
        if sig:
            self.cnt[sem] += sig
        return self.cnt[sem]


class _StopRecording(Exception):
    pass


def build(nt=NT, taps=(), stop_after=None, no_scratch=False):
    taps = list(taps)
    if stop_after is not None:
        if stop_after.startswith("!"):
            stop_after = stop_after[1:]
        elif stop_after not in taps:
            taps.append(stop_after)
    nc = bass.Bass("TRN2", target_bir_lowering=False, detect_race_conditions=False)
    p = Prog()

    x_d = nc.dram_tensor("x", [T, D], F32, kind="ExternalInput").ap()
    w_qkv = nc.dram_tensor("w_qkv", [D, 3 * D], F32, kind="ExternalInput").ap()
    b_qkv = nc.dram_tensor("b_qkv", [3 * D], F32, kind="ExternalInput").ap()
    w_gate = nc.dram_tensor("w_gate", [D, D], F32, kind="ExternalInput").ap()
    b_gate = nc.dram_tensor("b_gate", [D], F32, kind="ExternalInput").ap()
    w_proj = nc.dram_tensor("w_proj", [D, D], F32, kind="ExternalInput").ap()
    b_proj = nc.dram_tensor("b_proj", [D], F32, kind="ExternalInput").ap()
    ln_g = nc.dram_tensor("ln_g", [D], F32, kind="ExternalInput").ap()
    ln_b = nc.dram_tensor("ln_b", [D], F32, kind="ExternalInput").ap()
    router_W = nc.dram_tensor("router_W", [2, D, NE], F32, kind="ExternalInput").ap()
    router_b = nc.dram_tensor("router_b", [2, NE], F32, kind="ExternalInput").ap()
    gate_bias = nc.dram_tensor("gate_bias", [2, NE], F32, kind="ExternalInput").ap()
    exp_Wup = nc.dram_tensor("exp_Wup", [NE, D, D], F32, kind="ExternalInput").ap()
    exp_bup = nc.dram_tensor("exp_bup", [NE, D], F32, kind="ExternalInput").ap()
    exp_Wg = nc.dram_tensor("exp_Wg", [NE, D, D], F32, kind="ExternalInput").ap()
    exp_bg = nc.dram_tensor("exp_bg", [NE, D], F32, kind="ExternalInput").ap()
    exp_Wdown = nc.dram_tensor("exp_Wdown", [NE, D, D], F32, kind="ExternalInput").ap()
    exp_bdown = nc.dram_tensor("exp_bdown", [NE, D], F32, kind="ExternalInput").ap()
    share_Wup = nc.dram_tensor("share_Wup", [D, 2 * D], F32, kind="ExternalInput").ap()
    share_bup = nc.dram_tensor("share_bup", [2 * D], F32, kind="ExternalInput").ap()
    share_Wg = nc.dram_tensor("share_Wg", [D, D], F32, kind="ExternalInput").ap()
    share_bg = nc.dram_tensor("share_bg", [D], F32, kind="ExternalInput").ap()
    share_Wdown = nc.dram_tensor("share_Wdown", [D, D], F32, kind="ExternalInput").ap()
    share_bdown = nc.dram_tensor("share_bdown", [D], F32, kind="ExternalInput").ap()
    norm_w = nc.dram_tensor("norm_w", [2, D], F32, kind="ExternalInput").ap()
    out_d = nc.dram_tensor("out", [T, D], F32, kind="ExternalOutput").ap()

    ident_b = nc.inline_tensor(np.eye(128, dtype=ml_dtypes.bfloat16), "identb").ap()
    ident_f = nc.inline_tensor(np.eye(128, dtype=np.float32), "identf").ap()
    mrows = np.arange(128)[:, None]
    mcols = np.arange(896)[None, :]
    mask_np = (mcols >= mrows + 384).astype(ml_dtypes.bfloat16)
    mask_c = nc.inline_tensor(mask_np, "maskc").ap()

    def chunked(ap2d):
        return ap2d.rearrange("(c p) f -> p c f", p=128)

    WMAT = {}
    WMAT["wgate"] = chunked(w_gate)
    WMAT["wq"] = chunked(w_qkv[:, 0:D])
    WMAT["wk"] = chunked(w_qkv[:, D:2 * D])
    WMAT["wv"] = chunked(w_qkv[:, 2 * D:3 * D])
    WMAT["wproj"] = chunked(w_proj)
    for e in range(NE):
        WMAT[f"up{e}"] = chunked(exp_Wup[e])
        WMAT[f"wg{e}"] = chunked(exp_Wg[e])
        WMAT[f"dn{e}"] = chunked(exp_Wdown[e])
    WMAT["sup0"] = chunked(share_Wup[:, 0:D])
    WMAT["sup1"] = chunked(share_Wup[:, D:2 * D])
    WMAT["sg"] = chunked(share_Wg)
    WMAT["sdn"] = chunked(share_Wdown)
    WSCR = {k: nc.dram_tensor(f"wscr_{k}", [128, KC * D], BF16, kind="Internal").ap()
            for k in WMAT}

    tap_out = {}
    ctx = contextlib.ExitStack()
    with ctx:
        sb = lambda name, shape, dt: ctx.enter_context(nc.sbuf_tensor(name, shape, dt))
        X = [sb("X0", [128, 4 * D], F32), sb("X1", [128, 4 * D], F32)]
        XF = sb("XF", [128, 4 * D], F32)
        S = sb("S", [128, KC * D], F32)
        KS = sb("KS", [128, KC], F32)
        KSB = sb("KSB", [128, KC], BF16)
        BClng = sb("BClng", [128, D], F32)
        BClnb = sb("BClnb", [128, D], F32)
        BCnw = [sb("BCnw0", [128, D], F32), sb("BCnw1", [128, D], F32)]
        BCbproj = sb("BCbproj", [128, D], F32)
        BCbv = sb("BCbv", [128, D], F32)
        BCsbd2 = sb("BCsbd2", [128, D], F32)
        RB = sb("RB", [128, 12], F32)
        GB = sb("GB", [128, 12], F32)
        BQKV = sb("BQKV", [128, 16], F32)
        BGATE = sb("BGATE", [128, KC], F32)
        BUP = sb("BUP", [128, NE * KC], F32)
        BGE = sb("BGE", [128, NE * KC], F32)
        SBUP = sb("SBUP", [128, 2 * KC], F32)
        SBG = sb("SBG", [128, KC], F32)
        BDR = sb("BDR", [NE, D], F32)
        BDRB = sb("BDRB", [NE, D], BF16)
        RW = sb("RW", [128, 2 * KC * NE], BF16)
        SCR = sb("SCR", [128, 64], F32)
        G8 = sb("G8", [128, 32], F32)
        BIAS8 = sb("BIAS8", [128, 32], F32)
        M8 = sb("M8", [128, 32], F32)
        SEL = sb("SEL", [128, 24], F32)
        GSEL = sb("GSEL", [128, 24], F32)
        W6 = sb("W6", [128, 24], F32)
        EL1 = sb("EL1", [128, 512], F32)
        EL2 = sb("EL2", [128, 512], F32)
        GS = sb("GS", [128, 1024], F32)
        WT = sb("WT", [NE, 4 * 128], BF16)
        IDB = sb("IDB", [128, 128], BF16)
        IDF = sb("IDF", [128, 128], F32)
        MASK = sb("MASK", [128, 896], BF16)
        ONES1 = sb("ONES1", [128, 1], BF16)
        CONSTS = sb("CONSTS", [128, 2], F32)
        B1 = sb("B1", [128, KC * 512], BF16)
        B2 = sb("B2", [128, KC * 512], BF16)
        B3 = sb("B3", [128, KC * 512], BF16)
        B4 = sb("B4", [128, 4 * D], BF16)
        B5 = sb("B5", [128, 4 * D], BF16)
        SC = sb("SC", [128, 4 * 512], BF16)
        W0 = sb("W0", [128, KC * D], BF16)
        W1 = sb("W1", [128, KC * D], BF16)
        WSLOT = [W0, W1]

        PT0 = ctx.enter_context(nc.psum_tensor("PT0", [128, 1024], F32))
        PT1 = ctx.enter_context(nc.psum_tensor("PT1", [128, 1024], F32))
        PF0 = ctx.enter_context(nc.psum_tensor("PF0", [128, 512], F32))
        PF1 = ctx.enter_context(nc.psum_tensor("PF1", [128, 512], F32))
        PF2 = ctx.enter_context(nc.psum_tensor("PF2", [128, 512], F32))
        PSM = ctx.enter_context(nc.psum_tensor("PSM", [128, 512], F32))
        PTR = PSM[:, 0:256].bitcast(BF16)
        PDEN = PSM[:, 256:260]
        PRT = PSM[:, 260:284]
        PW6 = PSM[0:NE, 284:412]

        sems = {e: ctx.enter_context(nc.semaphore(f"sem_{e}")) for e in ALLSEMS}

        regions = {k: {"ap": a, "evac": None} for k, a in
                   [("PT0", PT0), ("PT1", PT1), ("PF0", PF0), ("PF1", PF1),
                    ("PF2", PF2), ("PTR", PTR), ("PDEN", PDEN), ("PRT", PRT),
                    ("PW6", PW6)]}

        def reg_start(name):
            r = regions[name]
            if r["evac"] is not None:
                p.wait("tensor", *r["evac"])
            return r["ap"]

        def reg_evac_done(name, eng):
            regions[name]["evac"] = (eng, p.cnt[eng])

        def t(fn, sig=0):
            return p.op("tensor", fn, sig)

        def v(fn, sig=0):
            return p.op("vector", fn, sig)

        def s(fn, sig=0):
            return p.op("scalar", fn, sig)

        def g(fn, sig=0):
            return p.op("gpsimd", fn, sig)

        def tap(lbl, ap, prod_eng):
            if lbl not in taps:
                if lbl == stop_after:
                    raise _StopRecording()
                return
            shp = list(ap.shape)
            t_d = nc.dram_tensor(f"tap_{lbl}", shp, F32, kind="ExternalOutput").ap()
            tap_out[lbl] = f"tap_{lbl}"
            p.wait("gpsimd", prod_eng, p.cnt[prod_eng])
            gv = p.op("gpsimd", lambda b, a=ap, o=t_d: b.dma_start(out=o, in_=a), sig=16, sem="dmi")
            for e in ("tensor", "vector", "scalar", "sync"):
                p.wait(e, "dmi", gv)
            if lbl == stop_after:
                raise _StopRecording()

        wuse_idx = [0]
        wdone = {}
        wwrite = {}
        wwrite_name = {}
        wcast = set()

        def w_use(name):
            idx = wuse_idx[0]
            wuse_idx[0] += 1
            slot = WSLOT[idx % 2]
            dw = ("dwa", "dwb")[idx % 2]
            ds = ("dsa", "dsb")[idx % 2]
            if name not in wcast or no_scratch:
                if not no_scratch:
                    wcast.add(name)
                if idx - 2 in wdone:
                    p.wait("gpsimd", "tensor", wdone[idx - 2])
                if idx - 2 in wwrite:
                    p.wait("gpsimd", ds, wwrite[idx - 2])
                src = WMAT[name]
                gv = p.op("gpsimd", lambda b, sl=slot, i=src:
                          b.dma_start(out=sl[:, :].rearrange("p (c f) -> p c f", c=KC), in_=i),
                          sig=16, sem=dw)
                p.wait("tensor", dw, gv)
                if not no_scratch:
                    p.wait("sync", dw, gv)
                    wwrite[idx] = p.op("sync", lambda b, sl=slot, o=WSCR[name]:
                                       b.dma_start(out=o, in_=sl[:, :]), sig=16, sem=ds)
                    wwrite_name[name] = (ds, wwrite[idx])
            else:
                if idx - 2 in wdone:
                    p.wait("sync", "tensor", wdone[idx - 2])
                p.wait("sync", *wwrite_name[name])
                sv = p.op("sync", lambda b, sl=slot, i=WSCR[name]:
                          b.dma_start(out=sl[:, :], in_=i), sig=16, sem=dw)
                p.wait("tensor", dw, sv)
            return idx, slot

        def w_done(idx):
            wdone[idx] = p.cnt["tensor"]

        def bcast_dma(dst, src1d):
            a = src1d.rearrange("(o f) -> o f", o=1).partition_broadcast(128)
            return p.op("sync", lambda b, dd=dst, i=a: b.dma_start(out=dd[:, :], in_=i), sig=16)

        bcast_dma(BClng, ln_g)
        bcast_dma(BClnb, ln_b)
        bcast_dma(BCnw[0], norm_w[0])
        bcast_dma(BCnw[1], norm_w[1])
        bcast_dma(BCbproj, b_proj)
        bcast_dma(BCbv, b_qkv[2 * D:3 * D])
        bcast_dma(BCsbd2, share_bdown)
        p.op("sync", lambda b: b.dma_start(
            out=RB[:, :], in_=router_b.rearrange("j e -> (j e)").rearrange("(o f) -> o f", o=1).partition_broadcast(128)), sig=16)
        p.op("sync", lambda b: b.dma_start(
            out=GB[:, :], in_=gate_bias.rearrange("j e -> (j e)").rearrange("(o f) -> o f", o=1).partition_broadcast(128)), sig=16)
        p.op("sync", lambda b: b.dma_start(out=BQKV[:, 0:8], in_=b_qkv[0:D].rearrange("(c p) -> p c", p=128), allow_slow_non_contiguous=True), sig=16)
        p.op("sync", lambda b: b.dma_start(out=BQKV[:, 8:16], in_=b_qkv[D:2 * D].rearrange("(c p) -> p c", p=128), allow_slow_non_contiguous=True), sig=16)
        p.op("sync", lambda b: b.dma_start(out=BGATE[:, :], in_=b_gate.rearrange("(c p) -> p c", p=128), allow_slow_non_contiguous=True), sig=16)
        p.op("sync", lambda b: b.dma_start(out=BUP[:, :].rearrange("p (e c) -> p e c", e=NE),
                                           in_=exp_bup.rearrange("e (c p) -> p e c", p=128), allow_slow_non_contiguous=True), sig=16)
        p.op("sync", lambda b: b.dma_start(out=BGE[:, :].rearrange("p (e c) -> p e c", e=NE),
                                           in_=exp_bg.rearrange("e (c p) -> p e c", p=128), allow_slow_non_contiguous=True), sig=16)
        p.op("sync", lambda b: b.dma_start(out=SBUP[:, :].rearrange("p (sl c) -> p sl c", sl=2),
                                           in_=share_bup.rearrange("(sl c p) -> p sl c", p=128, sl=2), allow_slow_non_contiguous=True), sig=16)
        p.op("sync", lambda b: b.dma_start(out=SBG[:, :], in_=share_bg.rearrange("(c p) -> p c", p=128), allow_slow_non_contiguous=True), sig=16)
        p.op("sync", lambda b: b.dma_start(out=BDR[:, :], in_=exp_bdown), sig=16)
        p.op("sync", lambda b: b.dma_start(out=IDB[:, :], in_=ident_b), sig=16)
        p.op("sync", lambda b: b.dma_start(out=IDF[:, :], in_=ident_f), sig=16)
        startup_sync = p.op("sync", lambda b: b.dma_start(out=MASK[:, :], in_=mask_c), sig=16)

        rw_src = router_W.rearrange("j (c p) e -> p j c e", p=128)
        p.op("gpsimd", lambda b: b.dma_start(
            out=RW[:, :].rearrange("p (j c e) -> p j c e", j=2, c=KC), in_=rw_src), sig=16, sem="dmi")
        rw_sig = p.op("gpsimd", lambda b: b.dma_start(out=BDRB[:, :], in_=exp_bdown), sig=16, sem="dmi")
        x4 = x_d.rearrange("(tl sbt pp) d -> tl pp sbt d", pp=128, sbt=NSUB)
        out4 = out_d.rearrange("(tl sbt pp) d -> tl pp sbt d", pp=128, sbt=NSUB)
        xload_sig = {0: p.op("gpsimd", lambda b: b.dma_start(
            out=X[0][:, :].rearrange("p (sbt d) -> p sbt d", sbt=NSUB), in_=x4[0]), sig=16, sem="dxi")}
        p.wait("tensor", "dmi", rw_sig)
        p.wait("vector", "dmi", rw_sig)
        p.wait("scalar", "dmi", rw_sig)

        p.wait("vector", "sync", startup_sync)
        p.op("vector", lambda b: b.tensor_scalar(out=BCsbd2[:, :], in0=BCsbd2[:, :],
                                                 scalar1=2.0, scalar2=None, op0=AL.mult))
        p.op("vector", lambda b: b.memset(S[:, :], 0.0))
        p.op("vector", lambda b: b.memset(KS[:, :], 0.0))
        p.op("vector", lambda b: b.memset(KSB[:, :], 0.0))
        p.op("vector", lambda b: b.memset(ONES1[:, :], 1.0))
        p.op("vector", lambda b: b.memset(CONSTS[:, 0:1], LN_EPS))
        p.op("vector", lambda b: b.memset(CONSTS[:, 1:2], RMS_EPS))
        p.op("vector", lambda b: b.memset(G8[:, :], -BIG))
        memv = p.op("vector", lambda b: b.memset(BIAS8[:, :], -BIG), sig=1)
        p.wait("vector", "sync", startup_sync)
        p.wait("scalar", "sync", startup_sync)
        p.wait("tensor", "sync", startup_sync)
        p.wait("tensor", "vector", memv)
        p.wait("scalar", "vector", memv)

        C_NMU, C_NMUD, C_SQ, C_RSTD = 0, 4, 8, 12
        C_DEN, C_REC, C_NM, C_SSUM, C_SREC = 16, 20, 24, 28, 32
        C_KSR = 36

        def sub_ap(buf, sub):
            return buf[:, sub * D:(sub + 1) * D]

        def emit_ln(src_buf, dst_buf, affine):
            for sub in range(NSUB):
                v(lambda b, a=sub_ap(src_buf, sub), o=SCR[:, C_NMU + sub:C_NMU + sub + 1]:
                  b.reduce_sum(o, a, axis=AX.X, negate=True))
            nmd = 0
            for sub in range(NSUB):
                nmd = v(lambda b, i=SCR[:, C_NMU + sub:C_NMU + sub + 1],
                        o=SCR[:, C_NMUD + sub:C_NMUD + sub + 1]:
                        b.tensor_scalar(out=o, in0=i, scalar1=1.0 / D, scalar2=None, op0=AL.mult),
                        sig=(1 if sub == NSUB - 1 else 0))
            p.wait("scalar", "vector", nmd)
            for sub in range(NSUB):
                s(lambda b, a=sub_ap(src_buf, sub), m=SCR[:, C_NMUD + sub:C_NMUD + sub + 1],
                  acc=SCR[:, C_SQ + sub:C_SQ + sub + 1]:
                  b.activation(out=GS[:, 0:D], in_=a, func=AF.Square, bias=m, accum_out=acc))
            for sub in range(NSUB):
                s(lambda b, i=SCR[:, C_SQ + sub:C_SQ + sub + 1]:
                  b.activation(out=i, in_=i, func=AF.Ln, scale=1.0 / D, bias=CONSTS[:, 0:1]))
            rs = 0
            for sub in range(NSUB):
                rs = s(lambda b, i=SCR[:, C_SQ + sub:C_SQ + sub + 1],
                       o=SCR[:, C_RSTD + sub:C_RSTD + sub + 1]:
                       b.activation(out=o, in_=i, func=AF.Exp, scale=-0.5),
                       sig=(1 if sub == NSUB - 1 else 0))
            p.wait("vector", "scalar", rs)
            subs = []
            for sub in range(NSUB):
                sg_ = v(lambda b, a=sub_ap(src_buf, sub), o=sub_ap(dst_buf, sub),
                        m=SCR[:, C_NMUD + sub:C_NMUD + sub + 1],
                        r=SCR[:, C_RSTD + sub:C_RSTD + sub + 1]:
                        b.tensor_scalar(out=o, in0=a, scalar1=m, scalar2=r,
                                        op0=AL.add, op1=AL.mult),
                        sig=(0 if affine else 1))
                if not affine:
                    subs.append(sg_)
            if affine:
                for sub in range(NSUB):
                    v(lambda b, a=sub_ap(dst_buf, sub): b.tensor_tensor(a, a, BClng[:, :], AL.mult))
                for sub in range(NSUB):
                    subs.append(v(lambda b, a=sub_ap(dst_buf, sub):
                                  b.tensor_tensor(a, a, BClnb[:, :], AL.add), sig=1))
            return subs

        b1_guard = [0]
        b2_guard = [0]

        def emit_to_fm(src_buf, dst_fm, v_ready):
            subs = v_ready if isinstance(v_ready, list) else [v_ready] * NSUB
            p.wait("scalar", "tensor", b2_guard[0])
            cast_sig = 0
            for sub in range(NSUB):
                p.wait("scalar", "vector", subs[sub])
                cast_sig = s(lambda b, a=sub_ap(src_buf, sub), o=B2[:, sub * D:(sub + 1) * D]:
                             b.copy(o, a), sig=(1 if sub == NSUB - 1 else 0))
            p.wait("tensor", "scalar", cast_sig)
            p.wait("scalar", "tensor", b1_guard[0])
            ev = 0
            for c in range(KC):
                ptr = reg_start("PTR")
                for j in range(NSUB):
                    t(lambda b, pa=ptr[:, j * 128:(j + 1) * 128],
                      ia=B2[:, j * D + c * 128: j * D + (c + 1) * 128]:
                      b.transpose(pa, ia, IDB[:, :]), sig=(1 if j == NSUB - 1 else 0))
                p.wait("scalar", "tensor", p.cnt["tensor"])
                ev = s(lambda b, pa=ptr[:, 0:512], o=dst_fm[:, c * 512:(c + 1) * 512]:
                       b.copy(o, pa), sig=1)
                reg_evac_done("PTR", "scalar")
            return ev

        def fm_group(psname, slot, o, rhs_fm):
            ps = reg_start(psname)
            sig = 0
            for c in range(KC):
                sig = t(lambda b, pa=ps[:, 0:512],
                        l=slot[:, c * D + o * 128: c * D + o * 128 + 128],
                        r=rhs_fm[:, c * 512: (c + 1) * 512],
                        st=(c == 0), sp=(c == KC - 1):
                        b.matmul(pa, l, r, start=st, stop=sp),
                        sig=(1 if c == KC - 1 else 0))
            return sig, ps

        PFROT = ["PF0", "PF1", "PF2"]

        outdma_sig = {}
        state_upd_sig = [0]
        ksb_write_sig = [0]
        gs_free = [0, 0]

        try:
         for ti in range(nt):
             cur = X[ti % 2]
             oth = X[(ti + 1) % 2]
             SB16 = oth[:, :].bitcast(BF16)
             first = (ti == 0)

             p.wait("vector", "dxi", xload_sig[ti])
             p.wait("scalar", "dxi", xload_sig[ti])
             p.wait("tensor", "dxi", xload_sig[ti])

             emit_ln(cur, oth, affine=False)
             xn_subs = emit_ln(oth, oth, affine=True)
             tap(f"xn{ti}", oth[:, :], "vector")
             xnfm_sig = emit_to_fm(oth, B1, xn_subs)
             tap(f"xnfm{ti}", B1[:, :], "scalar")

             if not first:
                 p.wait("scalar", "vector", state_upd_sig[0])
                 for c in range(KC):
                     s(lambda b, a=S[:, c * D:(c + 1) * D], o=SB16[:, c * D:(c + 1) * D]:
                       b.copy(o, a), sig=(1 if c == KC - 1 else 0))
             scast_sig = p.cnt["scalar"]

             p.wait("tensor", "scalar", xnfm_sig)
             tap(f"cp1_{ti}", SCR[:, 0:4], "vector")
             idx, slot = w_use("wgate")
             tap(f"wgslot{ti}", W0[:, :], "gpsimd")
             for o in range(KC):
                 sig, ps = fm_group(PFROT[o % 3], slot, o, B1)
                 p.wait("scalar", "tensor", sig)
                 s(lambda b, pa=ps[:, 0:512], bb=BGATE[:, o:o + 1], oo=B5[:, o * 512:(o + 1) * 512]:
                   b.activation(out=oo, in_=pa, func=AF.Sigmoid, bias=bb), sig=1)
                 reg_evac_done(PFROT[o % 3], "scalar")
             w_done(idx)
             sig_sigma = p.cnt["scalar"]
             tap(f"sig{ti}", B5[:, :], "scalar")

             idx, slot = w_use("wq")
             for o in range(KC):
                 sig, ps = fm_group(PFROT[o % 3], slot, o, B1)
                 p.wait("vector", "tensor", sig)
                 p.wait("vector", "tensor", b2_guard[0])
                 v(lambda b, pa=ps[:, 0:512], bb=BQKV[:, o:o + 1]:
                   b.tensor_scalar(out=EL1[:, :], in0=pa, scalar1=bb, scalar2=0.0,
                                   op0=AL.add, op1=AL.min))
                 vv = v(lambda b, pa=ps[:, 0:512], bb=BQKV[:, o:o + 1]:
                        b.tensor_scalar(out=EL2[:, :], in0=pa, scalar1=bb, scalar2=0.0,
                                        op0=AL.add, op1=AL.max), sig=1)
                 reg_evac_done(PFROT[o % 3], "vector")
                 p.wait("scalar", "vector", vv)
                 ss = s(lambda b: b.activation(out=EL1[:, :], in_=EL1[:, :], func=AF.Exp), sig=1)
                 p.wait("vector", "scalar", ss)
                 v(lambda b, oo=B2[:, o * 512:(o + 1) * 512]:
                   b.tensor_tensor(oo, EL1[:, :], EL2[:, :], AL.add), sig=1)
             w_done(idx)
             qf_sig = p.cnt["vector"]
             tap(f"qf{ti}", B2[:, :], "vector")

             idx, slot = w_use("wk")
             p.wait("vector", "scalar", sig_sigma)
             for o in range(KC):
                 sig, ps = fm_group(PFROT[o % 3], slot, o, B1)
                 p.wait("vector", "tensor", sig)
                 v(lambda b, pa=ps[:, 0:512], bb=BQKV[:, 8 + o:9 + o], sg=B5[:, o * 512:(o + 1) * 512]:
                   b.scalar_tensor_tensor(out=EL1[:, :], in0=pa, scalar=bb, in1=sg,
                                          op0=AL.add, op1=AL.mult))
                 v(lambda b: b.tensor_scalar(out=EL2[:, :], in0=EL1[:, :], scalar1=0.0,
                                             scalar2=None, op0=AL.min))
                 vv = v(lambda b: b.tensor_scalar(out=EL1[:, :], in0=EL1[:, :], scalar1=0.0,
                                                  scalar2=None, op0=AL.max), sig=1)
                 reg_evac_done(PFROT[o % 3], "vector")
                 p.wait("scalar", "vector", vv)
                 ss = s(lambda b: b.activation(out=EL2[:, :], in_=EL2[:, :], func=AF.Exp), sig=1)
                 p.wait("vector", "scalar", ss)
                 v(lambda b, oo=B3[:, o * 512:(o + 1) * 512]:
                   b.tensor_tensor(oo, EL2[:, :], EL1[:, :], AL.add), sig=1)
             w_done(idx)
             kf_sig = p.cnt["vector"]
             tap(f"kf{ti}", B3[:, :], "vector")

             idx, slot = w_use("wv")
             for j in range(NSUB):
                 psn = "PT0" if j % 2 == 0 else "PT1"
                 ps = reg_start(psn)
                 sig = 0
                 for h in range(2):
                     for c in range(KC):
                         sig = t(lambda b, pa=ps[:, h * 512:(h + 1) * 512],
                                 l=B1[:, c * 512 + j * 128: c * 512 + j * 128 + 128],
                                 r=slot[:, c * D + h * 512: c * D + h * 512 + 512],
                                 st=(c == 0), sp=(c == KC - 1):
                                 b.matmul(pa, l, r, start=st, stop=sp),
                                 sig=(1 if h == 1 and c == KC - 1 else 0))
                 p.wait("vector", "tensor", sig)
                 v(lambda b, pa=ps[:, 0:1024], oo=B4[:, j * D:(j + 1) * D]:
                   b.tensor_tensor(oo, pa, BCbv[:, :], AL.add), sig=1)
                 reg_evac_done(psn, "vector")
             w_done(idx)
             vproj_done = p.cnt["tensor"]
             b1_guard[0] = vproj_done
             vt_sig = p.cnt["vector"]
             tap(f"vt{ti}", B4[:, :], "vector")

             p.wait("tensor", "vector", kf_sig)
             for cs in range(NSUB):
                 psn = PFROT[cs % 3]
                 ps = reg_start(psn)
                 sig = 0
                 for c in range(KC):
                     sig = t(lambda b, pa=ps[:, 0:512],
                             l=B3[:, c * 512 + cs * 128: c * 512 + cs * 128 + 128],
                             r=B2[:, c * 512: (c + 1) * 512],
                             st=(c == 0), sp=(c == KC - 1):
                             b.matmul(pa, l, r, start=st, stop=sp),
                             sig=(1 if c == KC - 1 else 0))
                 p.wait("vector", "tensor", sig)
                 v(lambda b, pa=ps[:, 0:512], mk=MASK[:, 384 - 128 * cs: 384 - 128 * cs + 512],
                   oo=SC[:, cs * 512:(cs + 1) * 512]:
                   b.tensor_tensor(oo, pa, mk, AL.mult), sig=1)
                 reg_evac_done(psn, "vector")
             sc_sig = p.cnt["vector"]
             tap(f"sc{ti}", SC[:, :], "vector")

             p.wait("scalar", "vector", kf_sig)
             kt_sig = 0
             for cs in range(NSUB):
                 for half in range(2):
                     ptr = reg_start("PTR")
                     for jj in range(4):
                         c = half * 4 + jj
                         t(lambda b, pa=ptr[:, jj * 128:(jj + 1) * 128],
                           ia=B3[:, c * 512 + cs * 128: c * 512 + (cs) * 128 + 128]:
                           b.transpose(pa, ia, IDB[:, :]), sig=(1 if jj == 3 else 0))
                     p.wait("scalar", "tensor", p.cnt["tensor"])
                     kt_sig = s(lambda b, pa=ptr[:, 0:512],
                                o=B5[:, cs * D + half * 512: cs * D + half * 512 + 512]:
                                b.copy(o, pa), sig=1)
                     reg_evac_done("PTR", "scalar")
             tap(f"kt{ti}", B5[:, :], "scalar")

             p.wait("tensor", "vector", sc_sig)
             p.wait("tensor", "vector", vt_sig)
             if not first:
                 p.wait("tensor", "scalar", scast_sig)
             num_evac = 0
             for o in range(KC):
                 psn = PFROT[o % 3]
                 ps = reg_start(psn)
                 nmm = NSUB + (0 if first else KC)
                 k_i = 0
                 sig = 0
                 for cs in range(NSUB):
                     sig = t(lambda b, pa=ps[:, 0:512],
                             l=B4[:, cs * D + o * 128: cs * D + o * 128 + 128],
                             r=SC[:, cs * 512:(cs + 1) * 512],
                             st=(k_i == 0), sp=(k_i == nmm - 1):
                             b.matmul(pa, l, r, start=st, stop=sp),
                             sig=(1 if k_i == nmm - 1 else 0))
                     k_i += 1
                 if not first:
                     for c in range(KC):
                         sig = t(lambda b, pa=ps[:, 0:512],
                                 l=SB16[:, c * D + o * 128: c * D + o * 128 + 128],
                                 r=B2[:, c * 512:(c + 1) * 512],
                                 st=(k_i == 0), sp=(k_i == nmm - 1):
                                 b.matmul(pa, l, r, start=st, stop=sp),
                                 sig=(1 if k_i == nmm - 1 else 0))
                         k_i += 1
                 p.wait("scalar", "tensor", sig)
                 num_evac = s(lambda b, pa=ps[:, 0:512], o=B1[:, o * 512:(o + 1) * 512]:
                              b.copy(o, pa), sig=1)
                 reg_evac_done(psn, "scalar")
             tap(f"num{ti}", B1[:, :], "scalar")

             tap(f"ksb{ti}", KSB[:, :], "scalar")
             p.wait("tensor", "scalar", kt_sig)
             if not first:
                 p.wait("tensor", "scalar", ksb_write_sig[0])
             psd = reg_start("PDEN")
             den_mm = 0
             for j in range(NSUB):
                 nmm = (j + 1) + (0 if first else KC)
                 k_i = 0
                 last_overall = (j == NSUB - 1)
                 for cs in range(j + 1):
                     den_mm = t(lambda b, pa=psd[:, j:j + 1],
                                l=SC[:, cs * 512 + j * 128: cs * 512 + j * 128 + 128],
                                r=ONES1[:, 0:1],
                                st=(k_i == 0), sp=(k_i == nmm - 1):
                                b.matmul(pa, l, r, start=st, stop=sp),
                                sig=(1 if last_overall and k_i == nmm - 1 else 0))
                     k_i += 1
                 if not first:
                     for c in range(KC):
                         den_mm = t(lambda b, pa=psd[:, j:j + 1],
                                    l=B2[:, c * 512 + j * 128: c * 512 + j * 128 + 128],
                                    r=KSB[:, c:c + 1],
                                    st=(k_i == 0), sp=(k_i == nmm - 1):
                                    b.matmul(pa, l, r, start=st, stop=sp),
                                    sig=(1 if last_overall and k_i == nmm - 1 else 0))
                         k_i += 1
             b2_guard[0] = den_mm
             p.wait("vector", "tensor", den_mm)
             v(lambda b: b.tensor_copy(SCR[:, 60:64], SCR[:, 56:60]))
             v(lambda b: b.tensor_copy(SCR[:, 60:64], SCR[:, 56:60]))
             den_ts = v(lambda b: b.tensor_scalar(out=SCR[:, C_DEN:C_DEN + 4], in0=psd[:, 0:4],
                                                  scalar1=DEN_EPS, scalar2=None, op0=AL.add), sig=1)
             reg_evac_done("PDEN", "vector")
             p.wait("scalar", "vector", den_ts)
             s(lambda b: b.activation(out=SCR[:, C_REC:C_REC + 4], in_=SCR[:, C_DEN:C_DEN + 4],
                                      func=AF.Ln))
             den_ev = s(lambda b: b.activation(out=SCR[:, C_REC:C_REC + 4], in_=SCR[:, C_REC:C_REC + 4],
                                               func=AF.Exp, scale=-1.0), sig=1)
             tap(f"rec{ti}", SCR[:, C_DEN:C_DEN + 8], "scalar")
             tap_den = f"den{ti}"
             if tap_den in taps:
                 tap(tap_den, SCR[:, C_DEN:C_DEN + 4], "vector")

             p.wait("vector", "vector", 0)
             for c in range(KC):
                 v(lambda b, a=B3[:, c * 512:(c + 1) * 512], o=SCR[:, C_KSR + c:C_KSR + c + 1]:
                   b.reduce_sum(o, a, axis=AX.X))
             p.wait("tensor", "scalar", kt_sig)
             for o in range(KC):
                 psn = "PT0" if o % 2 == 0 else "PT1"
                 ps = reg_start(psn)
                 sig = 0
                 for h in range(2):
                     for cs in range(NSUB):
                         sig = t(lambda b, pa=ps[:, h * 512:(h + 1) * 512],
                                 l=B5[:, cs * D + o * 128: cs * D + o * 128 + 128],
                                 r=B4[:, cs * D + h * 512: cs * D + h * 512 + 512],
                                 st=(cs == 0), sp=(cs == NSUB - 1):
                                 b.matmul(pa, l, r, start=st, stop=sp),
                                 sig=(1 if h == 1 and cs == NSUB - 1 else 0))
                 p.wait("vector", "tensor", sig)
                 if not first:
                     p.wait("vector", "scalar", scast_sig)
                 v(lambda b, pa=ps[:, 0:1024], ss_=S[:, o * D:(o + 1) * D]:
                   b.tensor_tensor(ss_, ss_, pa, AL.add), sig=1)
                 reg_evac_done(psn, "vector")
             state_mm_done = p.cnt["tensor"]
             state_upd_sig[0] = v(lambda b: b.tensor_tensor(
                 KS[:, 0:KC], KS[:, 0:KC], SCR[:, C_KSR:C_KSR + KC], AL.add), sig=1)
             p.wait("scalar", "vector", state_upd_sig[0])
             p.wait("scalar", "tensor", den_mm)
             ksb_write_sig[0] = s(lambda b: b.copy(KSB[:, :], KS[:, :]), sig=1)
             if f"S{ti}" in taps:
                 tap(f"S{ti}", S[:, :], "vector")

             idx, slot = w_use("wproj")
             p.wait("tensor", "scalar", num_evac)
             for j in range(NSUB):
                 psn = "PT0" if j % 2 == 0 else "PT1"
                 ps = reg_start(psn)
                 sig = 0
                 for h in range(2):
                     for c in range(KC):
                         sig = t(lambda b, pa=ps[:, h * 512:(h + 1) * 512],
                                 l=B1[:, c * 512 + j * 128: c * 512 + j * 128 + 128],
                                 r=slot[:, c * D + h * 512: c * D + h * 512 + 512],
                                 st=(c == 0), sp=(c == KC - 1):
                                 b.matmul(pa, l, r, start=st, stop=sp),
                                 sig=(1 if h == 1 and c == KC - 1 else 0))
                 p.wait("vector", "tensor", sig)
                 p.wait("vector", "scalar", den_ev)
                 v(lambda b, pa=ps[:, 0:1024], xs=sub_ap(cur, j), rc=SCR[:, C_REC + j:C_REC + j + 1]:
                   b.scalar_tensor_tensor(out=xs, in0=pa, scalar=rc, in1=xs,
                                          op0=AL.mult, op1=AL.add))
                 v(lambda b, xs=sub_ap(cur, j): b.tensor_tensor(xs, xs, BCbproj[:, :], AL.add), sig=1)
                 reg_evac_done(psn, "vector")
             w_done(idx)
             proj_mm_done = p.cnt["tensor"]
             b1_guard[0] = proj_mm_done
             xp_sig = p.cnt["vector"]
             tap(f"xp{ti}", cur[:, :], "vector")

             if ti + 1 < nt:
                 p.wait("gpsimd", "tensor", state_mm_done)
                 p.wait("gpsimd", "scalar", scast_sig if not first else xnfm_sig)
                 xload_sig[ti + 1] = p.op("gpsimd", lambda b, dd=oth, i=x4[ti + 1]: b.dma_start(
                     out=dd[:, :].rearrange("p (sbt d) -> p sbt d", sbt=NSUB), in_=i), sig=16, sem="dxi")

             if ti > 0:
                 p.wait("vector", "dxo", outdma_sig[ti - 1])
             ln_subs = emit_ln(cur, XF, affine=False)
             tap(f"xf{ti}", XF[:, :], "vector")

             for jc in range(2):
                 xffm_sig = emit_to_fm(XF, B1, ln_subs)
                 tr_done = p.cnt["tensor"]

                 p.wait("tensor", "scalar", xffm_sig)
                 psr = reg_start("PRT")
                 rt_mm = 0
                 for j in range(NSUB):
                     for c in range(KC):
                         rt_mm = t(lambda b, pa=psr[:, j * 6:(j + 1) * 6],
                                   l=B1[:, c * 512 + j * 128: c * 512 + j * 128 + 128],
                                   r=RW[:, jc * KC * NE + c * NE: jc * KC * NE + (c + 1) * NE],
                                   st=(c == 0), sp=(c == KC - 1):
                                   b.matmul(pa, l, r, start=st, stop=sp),
                                   sig=(1 if j == NSUB - 1 and c == KC - 1 else 0))
                 p.wait("vector", "tensor", rt_mm)
                 rt_ev = 0
                 for j in range(NSUB):
                     rt_ev = v(lambda b, pa=psr[:, j * 6:(j + 1) * 6], o=G8[:, j * 8:j * 8 + 6],
                               rb=RB[:, jc * 6:(jc + 1) * 6]:
                               b.tensor_tensor(o, pa, rb, AL.add),
                               sig=(1 if j == NSUB - 1 else 0))
                 reg_evac_done("PRT", "vector")
                 for j in range(NSUB):
                     v(lambda b, i=G8[:, j * 8:j * 8 + 6], o=BIAS8[:, j * 8:j * 8 + 6],
                       gb=GB[:, jc * 6:(jc + 1) * 6]:
                       b.tensor_tensor(o, i, gb, AL.add))
                 for j in range(NSUB):
                     v(lambda b, i=BIAS8[:, j * 8:(j + 1) * 8], o=M8[:, j * 8:(j + 1) * 8]:
                       b.max(out=o, in_=i))
                 for j in range(NSUB):
                     v(lambda b, i=BIAS8[:, j * 8:j * 8 + 6], o=SEL[:, j * 6:(j + 1) * 6],
                       m2=M8[:, j * 8 + 1:j * 8 + 2]:
                       b.tensor_single_scalar(o, i, m2, AL.is_ge))
                 for j in range(NSUB):
                     v(lambda b, a=SEL[:, j * 6:(j + 1) * 6]:
                       b.tensor_scalar(out=a, in0=a, scalar1=1.0, scalar2=BIG,
                                       op0=AL.subtract, op1=AL.mult))
                 for j in range(NSUB):
                     v(lambda b, ga=G8[:, j * 8:j * 8 + 6], se=SEL[:, j * 6:(j + 1) * 6],
                       o=GSEL[:, j * 6:(j + 1) * 6]:
                       b.tensor_tensor(o, ga, se, AL.add))
                 for j in range(NSUB):
                     v(lambda b, i=GSEL[:, j * 6:(j + 1) * 6], o=SCR[:, C_NM + j:C_NM + j + 1]:
                       b.reduce_max(o, i, axis=AX.X, negate=True))
                 v(lambda b: b.tensor_copy(SCR[:, 60:64], SCR[:, C_NM:C_NM + 4]))
                 nm = v(lambda b: b.tensor_copy(SCR[:, 60:64], SCR[:, C_NM:C_NM + 4]), sig=1)
                 p.wait("scalar", "vector", nm)
                 es = 0
                 for j in range(NSUB):
                     es = s(lambda b, a=GSEL[:, j * 6:(j + 1) * 6], bi=SCR[:, C_NM + j:C_NM + j + 1]:
                            b.activation(out=a, in_=a, func=AF.Exp, bias=bi),
                            sig=(1 if j == NSUB - 1 else 0))
                 p.wait("vector", "scalar", es)
                 for j in range(NSUB):
                     v(lambda b, i=GSEL[:, j * 6:(j + 1) * 6], o=SCR[:, C_SSUM + j:C_SSUM + j + 1]:
                       b.reduce_sum(o, i, axis=AX.X))
                 v(lambda b: b.tensor_copy(SCR[:, 60:64], SCR[:, C_SSUM:C_SSUM + 4]))
                 ssum_sig = v(lambda b: b.tensor_copy(SCR[:, 60:64], SCR[:, C_SSUM:C_SSUM + 4]), sig=1)
                 p.wait("scalar", "vector", ssum_sig)
                 s(lambda b: b.activation(out=SCR[:, C_SREC:C_SREC + 4], in_=SCR[:, C_SSUM:C_SSUM + 4],
                                          func=AF.Ln))
                 srec_sig = s(lambda b: b.activation(out=SCR[:, C_SREC:C_SREC + 4], in_=SCR[:, C_SREC:C_SREC + 4],
                                                     func=AF.Exp, scale=-1.0), sig=1)
                 p.wait("vector", "scalar", srec_sig)
                 w6s = 0
                 for j in range(NSUB):
                     w6s = v(lambda b, i=GSEL[:, j * 6:(j + 1) * 6], o=W6[:, j * 6:(j + 1) * 6],
                             rc=SCR[:, C_SREC + j:C_SREC + j + 1]:
                             b.tensor_scalar(out=o, in0=i, scalar1=rc, scalar2=None, op0=AL.mult),
                             sig=(1 if j == NSUB - 1 else 0))
                 if f"w6_{ti}_{jc}" in taps:
                     tap(f"w6_{ti}_{jc}", W6[:, :], "vector")

                 for e in range(NE):
                     iu, slu = w_use(f"up{e}")
                     p.wait("scalar", "tensor", b2_guard[0])
                     h_ev = 0
                     for o in range(KC):
                         sig, ps = fm_group(PFROT[o % 3], slu, o, B1)
                         p.wait("scalar", "tensor", sig)
                         h_ev = s(lambda b, pa=ps[:, 0:512], bb=BUP[:, e * KC + o:e * KC + o + 1],
                                  oo=B2[:, o * 512:(o + 1) * 512]:
                                  b.activation(out=oo, in_=pa, func=AF.Identity, bias=bb), sig=1)
                         reg_evac_done(PFROT[o % 3], "scalar")
                     w_done(iu)
                     ig, slg = w_use(f"wg{e}")
                     p.wait("tensor", "scalar", h_ev)
                     prod_sig = 0
                     for o in range(KC):
                         sig, ps = fm_group(PFROT[o % 3], slg, o, B2)
                         p.wait("scalar", "tensor", sig)
                         p.wait("scalar", "vector", gs_free[o % 2])
                         gsig = s(lambda b, pa=ps[:, 0:512], bb=BGE[:, e * KC + o:e * KC + o + 1],
                                  oo=GS[:, (o % 2) * 512:(o % 2) * 512 + 512]:
                                  b.activation(out=oo, in_=pa, func=AF.Silu, bias=bb), sig=1)
                         reg_evac_done(PFROT[o % 3], "scalar")
                         p.wait("vector", "scalar", gsig)
                         prod_sig = v(lambda b, gg=GS[:, (o % 2) * 512:(o % 2) * 512 + 512],
                                      hh=B2[:, o * 512:(o + 1) * 512],
                                      oo=B3[:, o * 512:(o + 1) * 512]:
                                      b.tensor_tensor(oo, gg, hh, AL.mult), sig=1)
                         gs_free[o % 2] = prod_sig
                     w_done(ig)
                     idn, sld = w_use(f"dn{e}")
                     p.wait("tensor", "vector", prod_sig)
                     for j in range(NSUB):
                         psn = "PT0" if j % 2 == 0 else "PT1"
                         ps = reg_start(psn)
                         sig = 0
                         for h in range(2):
                             for c in range(KC):
                                 sig = t(lambda b, pa=ps[:, h * 512:(h + 1) * 512],
                                         l=B3[:, c * 512 + j * 128: c * 512 + j * 128 + 128],
                                         r=sld[:, c * D + h * 512: c * D + h * 512 + 512],
                                         st=(c == 0), sp=(c == KC - 1):
                                         b.matmul(pa, l, r, start=st, stop=sp),
                                         sig=(1 if h == 1 and c == KC - 1 else 0))
                         p.wait("vector", "tensor", sig)
                         v(lambda b, pa=ps[:, 0:1024], xs=sub_ap(XF, j),
                           ww=W6[:, j * 6 + e:j * 6 + e + 1]:
                           b.scalar_tensor_tensor(out=xs, in0=pa, scalar=ww, in1=xs,
                                                  op0=AL.mult, op1=AL.add), sig=1)
                         reg_evac_done(psn, "vector")
                     w_done(idn)
                 b2_guard[0] = p.cnt["tensor"]

                 iu0, sl0 = w_use("sup0")
                 p.wait("scalar", "tensor", state_mm_done)
                 h0_ev = 0
                 for o in range(KC):
                     sig, ps = fm_group(PFROT[o % 3], sl0, o, B1)
                     p.wait("scalar", "tensor", sig)
                     h0_ev = s(lambda b, pa=ps[:, 0:512], bb=SBUP[:, o:o + 1],
                               oo=B4[:, o * 512:(o + 1) * 512]:
                               b.activation(out=oo, in_=pa, func=AF.Identity, bias=bb), sig=1)
                     reg_evac_done(PFROT[o % 3], "scalar")
                 w_done(iu0)
                 iu1, sl1 = w_use("sup1")
                 h1_ev = 0
                 for o in range(KC):
                     sig, ps = fm_group(PFROT[o % 3], sl1, o, B1)
                     p.wait("scalar", "tensor", sig)
                     h1_ev = s(lambda b, pa=ps[:, 0:512], bb=SBUP[:, KC + o:KC + o + 1],
                               oo=B5[:, o * 512:(o + 1) * 512]:
                               b.activation(out=oo, in_=pa, func=AF.Identity, bias=bb), sig=1)
                     reg_evac_done(PFROT[o % 3], "scalar")
                 w_done(iu1)
                 b1_guard[0] = p.cnt["tensor"]

                 isg, slsg = w_use("sg")
                 p.wait("tensor", "scalar", h1_ev)
                 prod_sigs = [0, 0]
                 for sl_i, (hbuf, pbuf) in enumerate(((B4, B3), (B5, B2))):
                     for o in range(KC):
                         ps = reg_start(PFROT[o % 3])
                         sig = 0
                         for c in range(KC):
                             sig = t(lambda b, pa=ps[:, 0:512],
                                     l=slsg[:, c * D + o * 128: c * D + o * 128 + 128],
                                     r=hbuf[:, c * 512:(c + 1) * 512],
                                     st=(c == 0), sp=(c == KC - 1):
                                     b.matmul(pa, l, r, start=st, stop=sp),
                                     sig=(1 if c == KC - 1 else 0))
                         p.wait("scalar", "tensor", sig)
                         p.wait("scalar", "vector", gs_free[o % 2])
                         gsig = s(lambda b, pa=ps[:, 0:512], bb=SBG[:, o:o + 1],
                                  oo=GS[:, (o % 2) * 512:(o % 2) * 512 + 512]:
                                  b.activation(out=oo, in_=pa, func=AF.Silu, bias=bb), sig=1)
                         reg_evac_done(PFROT[o % 3], "scalar")
                         p.wait("vector", "scalar", gsig)
                         prod_sigs[sl_i] = v(lambda b, gg=GS[:, (o % 2) * 512:(o % 2) * 512 + 512],
                                             hh=hbuf[:, o * 512:(o + 1) * 512],
                                             oo=pbuf[:, o * 512:(o + 1) * 512]:
                                             b.tensor_tensor(oo, gg, hh, AL.mult), sig=1)
                         gs_free[o % 2] = prod_sigs[sl_i]
                 w_done(isg)

                 isd, slsd = w_use("sdn")
                 p.wait("tensor", "vector", prod_sigs[1])
                 for j in range(NSUB):
                     psn = "PT0" if j % 2 == 0 else "PT1"
                     ps = reg_start(psn)
                     sig = 0
                     for h in range(2):
                         for sl_i, pbuf in enumerate((B3, B2)):
                             for c in range(KC):
                                 sig = t(lambda b, pa=ps[:, h * 512:(h + 1) * 512],
                                         l=pbuf[:, c * 512 + j * 128: c * 512 + j * 128 + 128],
                                         r=slsd[:, c * D + h * 512: c * D + h * 512 + 512],
                                         st=(sl_i == 0 and c == 0), sp=(sl_i == 1 and c == KC - 1):
                                         b.matmul(pa, l, r, start=st, stop=sp),
                                         sig=(1 if h == 1 and sl_i == 1 and c == KC - 1 else 0))
                     p.wait("vector", "tensor", sig)
                     v(lambda b, pa=ps[:, 0:1024], xs=sub_ap(XF, j):
                       b.tensor_tensor(xs, xs, pa, AL.add))
                     v(lambda b, xs=sub_ap(XF, j):
                       b.tensor_tensor(xs, xs, BCsbd2[:, :], AL.add), sig=1)
                     reg_evac_done(psn, "vector")
                 w_done(isd)
                 b2_guard[0] = p.cnt["tensor"]

                 p.wait("tensor", "vector", w6s)
                 for j in range(NSUB):
                     psw = reg_start("PW6")
                     t(lambda b, pa=psw[:, 0:128], ia=W6[:, j * 6:(j + 1) * 6]:
                       b.transpose(pa, ia, IDF[:, :]), sig=1)
                     p.wait("scalar", "tensor", p.cnt["tensor"])
                     wts = s(lambda b, pa=psw[:, 0:128], o=WT[:, j * 128:(j + 1) * 128]:
                             b.copy(o, pa), sig=1)
                     reg_evac_done("PW6", "scalar")
                     psn = "PT0" if j % 2 == 0 else "PT1"
                     ps = reg_start(psn)
                     p.wait("tensor", "scalar", wts)
                     sig = 0
                     for h in range(2):
                         sig = t(lambda b, pa=ps[:, h * 512:(h + 1) * 512],
                                 l=WT[:, j * 128:(j + 1) * 128],
                                 r=BDRB[:, h * 512:(h + 1) * 512]:
                                 b.matmul(pa, l, r, start=True, stop=True),
                                 sig=(1 if h == 1 else 0))
                     p.wait("vector", "tensor", sig)
                     v(lambda b, pa=ps[:, 0:1024], xs=sub_ap(XF, j):
                       b.tensor_tensor(xs, xs, pa, AL.add), sig=1)
                     reg_evac_done(psn, "vector")

                 acc_sig = p.cnt["vector"]
                 p.wait("scalar", "vector", acc_sig)
                 for sub in range(NSUB):
                     s(lambda b, a=sub_ap(XF, sub), acc=SCR[:, C_SQ + sub:C_SQ + sub + 1]:
                       b.activation(out=GS[:, 0:D], in_=a, func=AF.Square, accum_out=acc))
                 for sub in range(NSUB):
                     s(lambda b, i=SCR[:, C_SQ + sub:C_SQ + sub + 1]:
                       b.activation(out=i, in_=i, func=AF.Ln, scale=1.0 / D, bias=CONSTS[:, 1:2]))
                 rs = 0
                 for sub in range(NSUB):
                     rs = s(lambda b, i=SCR[:, C_SQ + sub:C_SQ + sub + 1],
                            o=SCR[:, C_RSTD + sub:C_RSTD + sub + 1]:
                            b.activation(out=o, in_=i, func=AF.Exp, scale=-0.5),
                            sig=(1 if sub == NSUB - 1 else 0))
                 p.wait("vector", "scalar", rs)
                 ln_subs = []
                 for sub in range(NSUB):
                     ln_subs.append(v(lambda b, a=sub_ap(XF, sub),
                                      r=SCR[:, C_RSTD + sub:C_RSTD + sub + 1],
                                      nw=BCnw[jc][:, :]:
                                      b.scalar_tensor_tensor(out=a, in0=a, scalar=r, in1=nw,
                                                             op0=AL.mult, op1=AL.mult),
                                      sig=1))
                 if f"ffn{ti}_{jc}" in taps:
                     tap(f"ffn{ti}_{jc}", XF[:, :], "vector")

             fin = 0
             for sub in range(NSUB):
                 fin = v(lambda b, a=sub_ap(XF, sub), xx=sub_ap(cur, sub):
                         b.tensor_tensor(a, a, xx, AL.add),
                         sig=(1 if sub == NSUB - 1 else 0))
             p.wait("gpsimd", "vector", fin)
             outdma_sig[ti] = p.op("gpsimd", lambda b, i=XF, oo=out4[ti]: b.dma_start(
                 out=oo, in_=i[:, :].rearrange("p (sbt d) -> p sbt d", sbt=NSUB)), sig=16, sem="dxo")

        except _StopRecording:
            pass
        p.wait("gpsimd", "dxo", p.cnt["dxo"])
        p.wait("gpsimd", "dmi", p.cnt["dmi"])

        with nc.Block() as block:
            def runner(eng_name):
                def run(b):
                    for item in p.q[eng_name]:
                        if item[0] == "w":
                            b.wait_ge(sems[item[1]], item[2])
                        else:
                            inst = item[1](b)
                            if item[2]:
                                inst.then_inc(sems[item[3]], item[2])
                return run
            block.tensor(runner("tensor"))
            block.vector(runner("vector"))
            block.scalar(runner("scalar"))
            block.sync(runner("sync"))
            block.gpsimd(runner("gpsimd"))

    return nc, tap_out


def kernel(**inputs):
    nc, _ = build()
    base = {
        "w_qkv": inputs["L0_attn_w_qkv"], "b_qkv": inputs["L0_attn_b_qkv"],
        "w_gate": inputs["L0_attn_w_gate"], "b_gate": inputs["L0_attn_b_gate"],
        "w_proj": inputs["L0_attn_w_proj"], "b_proj": inputs["L0_attn_b_proj"],
        "ln_g": inputs["L0_attn_ln_g"], "ln_b": inputs["L0_attn_ln_b"],
        "router_W": inputs["L0_ffn_router_W"], "router_b": inputs["L0_ffn_router_b"],
        "gate_bias": inputs["L0_ffn_gate_bias"],
        "exp_Wup": inputs["L0_ffn_exp_Wup"], "exp_bup": inputs["L0_ffn_exp_bup"],
        "exp_Wg": inputs["L0_ffn_exp_Wg"], "exp_bg": inputs["L0_ffn_exp_bg"],
        "exp_Wdown": inputs["L0_ffn_exp_Wdown"], "exp_bdown": inputs["L0_ffn_exp_bdown"],
        "share_Wup": inputs["L0_ffn_share_Wup"], "share_bup": inputs["L0_ffn_share_bup"],
        "share_Wg": inputs["L0_ffn_share_Wg"], "share_bg": inputs["L0_ffn_share_bg"],
        "share_Wdown": inputs["L0_ffn_share_Wdown"], "share_bdown": inputs["L0_ffn_share_bdown"],
        "norm_w": inputs["L0_ffn_norm_w"],
    }
    base = {k: np.ascontiguousarray(v, dtype=np.float32) for k, v in base.items()}
    xfull = np.ascontiguousarray(inputs["L0_attn_x"], dtype=np.float32)
    in_maps = []
    for c in range(NCORES):
        m = dict(base)
        m["x"] = np.ascontiguousarray(xfull[c])
        in_maps.append(m)
    res = run_bass_kernel_spmd(nc, in_maps, core_ids=list(range(NCORES)))
    return np.stack([r["out"] for r in res.results], axis=0)
